# version 08
# baseline (speedup 1.0000x reference)
import contextlib

import numpy as np
import ml_dtypes

import concourse.bass as bass
import concourse.mybir as mybir
from concourse.bass_utils import run_bass_kernel_spmd

F32 = mybir.dt.float32
BF16 = mybir.dt.bfloat16
U8 = mybir.dt.uint8
AF = mybir.ActivationFunctionType
ALU = mybir.AluOpType
AX = mybir.AxisListType.X
bf16 = ml_dtypes.bfloat16

D = 1024
H = 16
E = 64
DH = 256
NE = 8
INTER = 1024
NSH = 2048
CH = 128
NC_D = D // 128
SLAB = 512
NT = SLAB // 128
NJ = SLAB // CH
NSHC = NSH // 128
NIC = INTER // 128

ACLAMP = 78.0
LN9995 = float(np.log(0.9995))

NWALL = 68


def g_qkv(wi, hw):
    return 2 * wi + hw


def g_sw1(i):
    return 6 + 2 * i


def g_sw3(i):
    return 7 + 2 * i


def g_sw2(i):
    return 14 + i


def g_we(e, kind, i):
    base = 18 + 6 * e
    if kind == 1:
        return base + 2 * i
    if kind == 3:
        return base + 2 * i + 1
    return base + 4 + i


def g_wo(i):
    return 66 + i


ENGS = ("PE", "DVE", "ACT", "POOL", "SP")
QLANES = {"qW": 10, "qX": 3, "qO": 3, "qC": 6}
QUEUES = tuple(f"{q}{i}" for q, n in QLANES.items() for i in range(n))


class Prog:

    def __init__(self, nc, es, pelag=96):
        self.nc = nc
        self.pelag = pelag
        self.sems = {}
        for name in ENGS + QUEUES:
            self.sems[name] = es.enter_context(nc.semaphore(name))
        self.cnt = {name: 0 for name in ENGS + QUEUES}
        self.qrr = {q: 0 for q in QLANES}
        self.streams = {e: [] for e in ENGS}
        self.last_w = {}
        self.readers = {}
        self.waited = {e: {} for e in ENGS}
        self.mm_total = 0
        self.mm_since_inc = 0
        self.pe_incs = []
        self.pe_wait_idx = 0
        self._cur = None

    def _rec(self, fn):
        self._cur[1].append([fn, None])

    def _mark_inc(self, sem_name, delta):
        assert self._cur[1] and self._cur[1][-1][1] is None
        self.cnt[sem_name] += delta
        self._cur[1][-1][1] = (sem_name, delta)

    def op(self, name, *a, **k):
        self._rec(lambda h: getattr(h, name)(*a, **k))

    def wait(self, sem_name, val):
        eng = self._cur[0]
        if self.waited[eng].get(sem_name, 0) >= val:
            return
        self.waited[eng][sem_name] = val
        s = self.sems[sem_name]
        self._rec(lambda h: h.wait_ge(s, val))

    def mm(self, *a, **k):
        assert self._cur[0] == "PE"
        kind = k.pop("kind", "matmul")
        w = 4 if kind == "transpose" else 1
        while (self.pe_wait_idx < len(self.pe_incs)
               and self.pe_incs[self.pe_wait_idx][1]
               <= self.mm_total - self.pelag):
            self.pe_wait_idx += 1
        if self.pe_wait_idx > 0:
            self.wait("PE", self.pe_incs[self.pe_wait_idx - 1][0])
        self._rec(lambda h: getattr(h, kind)(*a, **k))
        self.mm_total += w
        self.mm_since_inc += w
        if self.mm_since_inc >= 32:
            self._mark_inc("PE", 1)
            self.pe_incs.append((self.cnt["PE"], self.mm_total))
            self.mm_since_inc = 0

    def transpose(self, out, in_, ident):
        self.mm(out, in_, ident, kind="transpose")

    def _deps(self, rd, wr):
        need = {}

        def add(t):
            if t:
                need[t[0]] = max(need.get(t[0], 0), t[1])

        for r in rd:
            add(self.last_w.get(r))
        for r in wr:
            add(self.last_w.get(r))
            for t in self.readers.get(r, ()):
                add(t)
        return need

    @contextlib.contextmanager
    def step(self, eng, rd=(), wr=(), q=None):
        assert self._cur is None
        ops = []
        self._cur = (eng, ops)
        if q:
            lane = f"{q}{self.qrr[q]}"
            self.qrr[q] = (self.qrr[q] + 1) % QLANES[q]
            self.wait(lane, self.cnt[lane])
            sem = lane
        else:
            sem = eng
        for s, v in sorted(self._deps(rd, wr).items()):
            self.wait(s, v)
        yield self
        if ops and ops[-1][1] is None:
            self._mark_inc(sem, 16 if q else 1)
            if eng == "PE" and not q and self.mm_since_inc:
                self.pe_incs.append((self.cnt["PE"], self.mm_total))
                self.mm_since_inc = 0
        token = (sem, self.cnt[sem])
        for r in rd:
            self.readers.setdefault(r, []).append(token)
        for r in wr:
            self.last_w[r] = token
            self.readers[r] = []
        self.streams[eng].extend(ops)
        self._cur = None

    def replay(self, eng, h):
        for fn, inc in self.streams[eng]:
            ins = fn(h)
            if inc is not None:
                ins.then_inc(self.sems[inc[0]], inc[1])


def build(cfg):
    T = cfg["T"]
    B = cfg["B"]
    assert T % SLAB == 0
    NSLAB = T // SLAB * B
    TOK = B * T
    NRING = cfg.get("nring", 5)

    nc = bass.Bass("TRN2", target_bir_lowering=False)

    def din(name, shape, dt=F32):
        return nc.dram_tensor(name, shape, dt, kind="ExternalInput").ap()

    x_d = din("x", [TOK, D])
    wall_d = din("wall", [NWALL, 128, 4096])
    wsm_d = din("wsm", [128, 2048 + 32 + 64])
    c_d = {nm: din(nm, sh, dt) for nm, sh, dt in [
        ("ident", [128, 128], BF16), ("u128", [128, 128], U8),
        ("zbf", [128, 128], BF16), ("sel16", [H, D], F32),
        ("b1c", [128, 2], F32), ("b2c", [H, 1], F32),
        ("gbias", [128, NE], F32), ("epsc", [128, 1], F32),
        ("decf", [128, NC_D], F32), ("bgate", [128, SLAB], F32)]}
    out_d = nc.dram_tensor("out", [TOK, D], F32, kind="ExternalOutput").ap()

    wallb = nc.dram_tensor("wallb", [NWALL, 128, 4096], BF16,
                           kind="Internal").ap()
    wsmb = nc.dram_tensor("wsmb", [128, 2048 + 32 + 64], BF16,
                          kind="Internal").ap()

    es = contextlib.ExitStack()
    with es:
        def sb(name, shape, dt=F32):
            return es.enter_context(nc.sbuf_tensor(name, shape, dt))

        ring = [sb(f"ring{i}", [128, 4096], BF16) for i in range(NRING)]
        xn_fm = [sb(f"xnfm{i}", [128, NC_D * SLAB], BF16) for i in range(2)]
        q_fm = sb("q_fm", [128, NC_D * SLAB], BF16)
        k_fm = sb("k_fm", [128, NC_D * SLAB], BF16)
        v_fm = sb("v_fm", [128, NC_D * SLAB], BF16)
        qa_fm = sb("qa_fm", [128, NC_D * SLAB], BF16)
        ka_fm = sb("ka_fm", [128, NC_D * SLAB], BF16)
        scol_all = sb("scol_all", [128, NC_D * NJ])
        L_fm = sb("L_fm", [128, NC_D * SLAB])
        y_fm = sb("y_fm", [128, NC_D * SLAB], BF16)
        hsh = sb("hsh", [128, NSHC * SLAB], BF16)
        yacc = sb("yacc", [128, NT * D])
        xa = [sb(f"xa{i}", [128, D]) for i in range(2)]
        xc = sb("xc", [128, D])
        sqs = sb("sqs", [128, D])
        xn_tm = [sb(f"xntm{i}", [128, D], BF16) for i in range(2)]
        z1t = sb("z1t", [128, 2 * SLAB], BF16)
        t2t = sb("t2t", [H, SLAB])
        lamw3 = sb("lamw3", [128, SLAB])
        lam4w = sb("lam4w", [128, 2 * SLAB])
        sa4w = sb("sa4w", [128, 2 * SLAB], BF16)
        col = [[sb(f"col{p}_{i}", [128, 1]) for i in range(8)]
               for p in range(2)]
        kTs = [sb(f"kT{i}", [128, CH], BF16) for i in range(2)]
        vTs = [sb(f"vT{i}", [128, CH], BF16) for i in range(2)]
        am0 = [sb(f"am0_{i}", [128, CH], BF16) for i in range(2)]
        am1 = [sb(f"am1_{i}", [128, CH], BF16) for i in range(2)]
        Sz0 = [sb(f"Sz0_{i}", [128, E], BF16) for i in range(2)]
        Sz1 = [sb(f"Sz1_{i}", [128, E], BF16) for i in range(2)]
        S_fp = [sb(f"S_fp{c}", [128, E]) for c in range(NC_D)]
        hsl = sb("hsl", [128, SLAB], BF16)
        gtmp = sb("gtmp", [128, D])
        gscr = sb("gscr", [128, NE])
        gat8 = sb("gat8", [128, NE])
        gsb = sb("gsb", [128, NE])
        gmsk = sb("gmsk", [128, NE])
        gm2 = sb("gm2", [128, NE])
        gatdb = sb("gatdb", [128, NT * NE])
        gc = [sb(f"gc{i}", [128, 1]) for i in range(6)]
        cs1r = sb("cs1r", [128, 2048], BF16)
        cs2r = sb("cs2r", [128, 32], BF16)
        gwr = sb("gwr", [128, 64], BF16)
        ident = sb("identc", [128, 128], BF16)
        u128 = sb("u128c", [128, 128], U8)
        zbf = sb("zbfc", [128, 128], BF16)
        sel16 = sb("sel16c", [H, D])
        b1c = sb("b1cc", [128, 2])
        b2c = sb("b2cc", [H, 1])
        gbias = sb("gbiasc", [128, NE])
        epsc = sb("epscc", [128, 1])
        decf = sb("decfc", [128, NC_D])
        bgate = sb("bgatec", [128, SLAB])
        basec = sb("basec", [128, NC_D])
        s1c = sb("s1c", [128, NC_D])
        s2c = sb("s2c", [128, NC_D])
        b2h = sb("b2h", [H, 1])

        psP = es.enter_context(nc.psum_tensor("psP", [128, 512], F32))
        psQ = es.enter_context(nc.psum_tensor("psQ", [128, 512], F32))
        psY = es.enter_context(nc.psum_tensor("psY", [128, 1024], F32))
        psL = es.enter_context(nc.psum_tensor("psL", [128, 512], F32))
        psTF = es.enter_context(nc.psum_tensor("psTF", [128, 512], BF16))
        psTS = es.enter_context(nc.psum_tensor("psTS", [128, 512], BF16))
        psAA = es.enter_context(nc.psum_tensor("psAA", [128, 512], F32))
        psA0 = psAA[:, 0:128]
        psA1 = psAA[:, 128:256]
        psO = psAA[:, 256:384]
        psS = psAA[:, 384:448]

        P = Prog(nc, es)

        items = [("ident", ident), ("u128", u128), ("zbf", zbf),
                 ("sel16", sel16), ("b1c", b1c), ("b2c", b2c),
                 ("gbias", gbias), ("epsc", epsc), ("decf", decf),
                 ("bgate", bgate)]
        for nm, tile in items:
            with P.step("SP", wr=(f"const_{nm}",), q="qW"):
                P.op("dma_start", out=tile[:], in_=c_d[nm])
        with P.step("SP", rd=tuple(f"const_{nm}" for nm, _ in items),
                    wr=("consts",)):
            P.op("nop")

        with P.step("ACT", rd=("consts",), wr=("basec",)):
            P.op("activation", basec[:], decf[:], AF.Sigmoid)
        with P.step("DVE", rd=("basec",), wr=("s1c",)):
            P.op("tensor_scalar_mul", s1c[:], basec[:], 0.1)
        with P.step("DVE", rd=("basec",), wr=("s2c",)):
            P.op("tensor_scalar_mul", s2c[:], basec[:], 1.1)
        with P.step("DVE", rd=("consts",), wr=("b2h",)):
            P.op("tensor_scalar_mul", b2h[:], b2c[:], 0.5)
        for r in range(2):
            with P.step("DVE", wr=(f"Sz0_{r}",)):
                P.op("memset", Sz0[r][:], 0.0)
            with P.step("DVE", wr=(f"Sz1_{r}",)):
                P.op("memset", Sz1[r][:], 0.0)

        with P.step("POOL", wr=("cgs",), q="qC"):
            P.op("dma_start", out=wsmb, in_=wsm_d)
        for g in range(NWALL):
            with P.step("POOL", wr=(f"cg{g}",), q="qC"):
                P.op("dma_start", out=wallb[g], in_=wall_d[g])

        with P.step("SP", rd=("cgs",), wr=("cs1r",), q="qW"):
            P.op("dma_start", out=cs1r[:], in_=wsmb[:, 0:2048])
        with P.step("SP", rd=("cgs",), wr=("cs2r",), q="qW"):
            P.op("dma_start", out=cs2r[:], in_=wsmb[:, 2048:2080])
        with P.step("SP", rd=("cgs",), wr=("gwr",), q="qW"):
            P.op("dma_start", out=gwr[:], in_=wsmb[:, 2080:2144])

        ring_i = [0]

        def load_gran(g):
            si = ring_i[0] % NRING
            ring_i[0] += 1
            slot = ring[si]
            keys = (f"rg{si}a", f"rg{si}b")
            for h in range(2):
                with P.step("SP", rd=(f"cg{g}",), wr=(keys[h],), q="qW"):
                    P.op("dma_start", out=slot[h * 64:(h + 1) * 64, :],
                         in_=wallb[g][h * 64:(h + 1) * 64, :])
            return slot, keys

        def fm(tile, c, lo, n):
            return tile[:, c * SLAB + lo: c * SLAB + lo + n]

        def evac(rd, wr, dst_ap, src_ap):
            with P.step("DVE", rd=rd, wr=wr):
                P.op("tensor_copy", dst_ap, src_ap)


        def gen_front(s):
            sp = s % 2
            sl0 = (s // (T // SLAB)) * T + (s % (T // SLAB)) * SLAB
            XW = (f"xn{sp}h0", f"xn{sp}h1")
            for Ti in range(NT):
                p = Ti % 2
                xt = xa[p]
                xkey = f"xa{p}"
                cl = col[p]
                ck = [f"c{p}_{i}" for i in range(8)]
                with P.step("POOL", wr=(xkey,), q="qX"):
                    P.op("dma_start", out=xt[:],
                         in_=x_d[sl0 + Ti * 128: sl0 + (Ti + 1) * 128, :])
                yield
                with P.step("DVE", rd=(xkey,), wr=(ck[0],)):
                    P.op("reduce_sum", cl[0][:], xt[:], axis=AX)
                yield
                with P.step("DVE", rd=(xkey,), wr=("sqs",)):
                    P.op("tensor_mul", sqs[:], xt[:], xt[:])
                yield
                with P.step("DVE", rd=("sqs",), wr=(ck[1],)):
                    P.op("reduce_sum", cl[1][:], sqs[:], axis=AX)
                yield
                with P.step("DVE", rd=(ck[0],), wr=(ck[2],)):
                    P.op("tensor_scalar_mul", cl[2][:], cl[0][:], 1.0 / D)
                with P.step("DVE", rd=(ck[1],), wr=(ck[3],)):
                    P.op("tensor_scalar_mul", cl[3][:], cl[1][:], 1.0 / D)
                with P.step("DVE", rd=(ck[2],), wr=(ck[4],)):
                    P.op("tensor_mul", cl[4][:], cl[2][:], cl[2][:])
                with P.step("DVE", rd=(ck[3], ck[4]), wr=(ck[5],)):
                    P.op("tensor_sub", cl[5][:], cl[3][:], cl[4][:])
                yield
                with P.step("ACT", rd=(ck[5], "consts"), wr=(ck[6],)):
                    P.op("activation", cl[6][:], cl[5][:], AF.Ln,
                         bias=epsc[:])
                with P.step("ACT", rd=(ck[6],), wr=(ck[7],)):
                    P.op("activation", cl[7][:], cl[6][:], AF.Exp,
                         scale=-0.5)
                yield
                with P.step("DVE", rd=(xkey, ck[2], ck[7]),
                            wr=(f"xntm{p}",)):
                    P.op("tensor_scalar", xn_tm[p][:], xt[:], cl[2][:],
                         cl[7][:], ALU.subtract, ALU.mult)
                yield
                for half in range(2):
                    pk = "psTF"
                    pt = psTF[:]
                    with P.step("PE", rd=(f"xntm{p}", "consts"), wr=(pk,)):
                        for cc in range(4):
                            c = half * 4 + cc
                            P.transpose(pt[:, cc * 128:(cc + 1) * 128],
                                        xn_tm[p][:, c * 128:(c + 1) * 128],
                                        ident[:])
                    yield
                    dst = xn_fm[sp][:].rearrange(
                        "p (c n) -> p c n", c=NC_D)[
                        :, half * 4:(half + 1) * 4, Ti * 128:(Ti + 1) * 128]
                    src = pt.rearrange("p (c n) -> p c n", c=4)
                    evac((pk,), (XW[half],), dst, src)
                    yield

        def gen_serial(s):
            skip = cfg.get("skip", ())
            sp = s % 2
            XN = (f"xn{sp}h0", f"xn{sp}h1")
            if s % (T // SLAB) == 0:
                for c in range(NC_D):
                    with P.step("DVE", wr=(f"S{c}",)):
                        P.op("memset", S_fp[c][:], 0.0)
                yield
            for wi, (dsttile, dkey) in () if "p2" in skip else enumerate(
                    ((q_fm, "q_fm"), (k_fm, "k_fm"), (v_fm, "v_fm"))):
                for hw in range(2):
                    slot, keys = load_gran(g_qkv(wi, hw))
                    yield
                    for cp4 in range(4):
                        cp = hw * 4 + cp4
                        with P.step("PE", rd=keys + XN, wr=("psL",)):
                            for c in range(NC_D):
                                P.mm(psL[:, 0:SLAB],
                                     slot[:, c * 512 + cp4 * 128:
                                          c * 512 + (cp4 + 1) * 128],
                                     fm(xn_fm[sp], c, 0, SLAB),
                                     start=(c == 0), stop=(c == NC_D - 1))
                        yield
                        evac(("psL",), (dkey,), fm(dsttile, cp, 0, SLAB),
                             psL[:, 0:SLAB])
                        yield
            if "p3" in skip:
                return
            for mp in range(2):
                with P.step("PE", rd=("cs1r",) + XN, wr=("psL",)):
                    for c in range(NC_D):
                        P.mm(psL[:, 0:SLAB],
                             cs1r[:, c * 256 + mp * 128:
                                  c * 256 + (mp + 1) * 128],
                             fm(xn_fm[sp], c, 0, SLAB),
                             start=(c == 0), stop=(c == NC_D - 1))
                yield
                with P.step("ACT", rd=("psL", "consts"), wr=(f"z1t{mp}",)):
                    P.op("activation", z1t[:, mp * SLAB:(mp + 1) * SLAB],
                         psL[:, 0:SLAB], AF.Tanh, bias=b1c[:, mp:mp + 1])
                yield
            with P.step("PE", rd=("cs2r", "z1t0", "z1t1"), wr=("psL",)):
                for c2 in range(2):
                    P.mm(psL[0:H, 0:SLAB], cs2r[:, c2 * H:(c2 + 1) * H],
                         z1t[:, c2 * SLAB:(c2 + 1) * SLAB],
                         start=(c2 == 0), stop=(c2 == 1))
            yield
            with P.step("ACT", rd=("psL", "b2h"), wr=("t2t",)):
                P.op("activation", t2t[:], psL[0:H, 0:SLAB], AF.Tanh,
                     bias=b2h[:], scale=0.5)
            yield
            for c in range(NC_D):
                with P.step("PE", rd=("t2t", "consts"), wr=("psL",)):
                    P.mm(psL[:, 0:SLAB], sel16[:, c * 128:(c + 1) * 128],
                         t2t[:], start=True, stop=True)
                yield
                with P.step("DVE", rd=("psL", "s1c", "s2c"), wr=("lamw3",)):
                    P.op("tensor_scalar", lamw3[:], psL[:, 0:SLAB],
                         s1c[:, c:c + 1], s2c[:, c:c + 1],
                         ALU.mult, ALU.add)
                yield
                with P.step("ACT", rd=("lamw3",), wr=("lamw3",)):
                    P.op("activation", lamw3[:], lamw3[:], AF.Ln)
                yield
                with P.step("DVE", rd=("lamw3",), wr=("lamw3",)):
                    P.op("tensor_scalar_min", lamw3[:], lamw3[:], LN9995)
                yield
                with P.step("DVE", rd=("lamw3", "consts"), wr=(f"L{c}",)):
                    P.op("tensor_tensor_scan", fm(L_fm, c, 0, SLAB),
                         bgate[:], lamw3[:], 0.0, ALU.mult, ALU.add)
                yield
            if "p45" in skip:
                return
            for c in range(NC_D):
                Lc = fm(L_fm, c, 0, SLAB)
                Lv = Lc.rearrange("p (j u) -> p j u", j=NJ)
                bc_mid = bass.AP(L_fm, c * SLAB + CH // 2 - 1,
                                 [[NC_D * SLAB, 128], [CH, NJ], [0, CH]])
                bc_end = bass.AP(L_fm, c * SLAB + CH - 1,
                                 [[NC_D * SLAB, 128], [CH, NJ], [0, CH]])
                lamv = lam4w[:, 0:SLAB].rearrange("p (j u) -> p j u", j=NJ)
                with P.step("DVE", rd=(f"L{c}",), wr=("lam4",)):
                    P.op("tensor_tensor", lamv, Lv, bc_mid, ALU.subtract)
                yield
                with P.step("DVE", rd=("lam4",), wr=("lam4",)):
                    P.op("tensor_scalar", lam4w[:, 0:SLAB], lam4w[:, 0:SLAB],
                         ACLAMP, -ACLAMP, ALU.min, ALU.max)
                yield
                with P.step("DVE", rd=("lam4",), wr=("lam4",)):
                    P.op("tensor_scalar_mul", lam4w[:, SLAB:2 * SLAB],
                         lam4w[:, 0:SLAB], -1.0)
                yield
                with P.step("ACT", rd=("lam4",), wr=("sa4",)):
                    P.op("activation", sa4w[:], lam4w[:], AF.Exp)
                yield
                with P.step("DVE", rd=("sa4", "q_fm"), wr=("qa_fm",)):
                    P.op("tensor_mul", fm(qa_fm, c, 0, SLAB),
                         fm(q_fm, c, 0, SLAB), sa4w[:, 0:SLAB])
                yield
                with P.step("DVE", rd=("sa4", "k_fm"), wr=("ka_fm",)):
                    P.op("tensor_mul", fm(ka_fm, c, 0, SLAB),
                         fm(k_fm, c, 0, SLAB), sa4w[:, SLAB:2 * SLAB])
                yield
                lamv2 = lam4w[:, 0:SLAB].rearrange("p (j u) -> p j u", j=NJ)
                with P.step("DVE", rd=(f"L{c}",), wr=("lam4",)):
                    P.op("tensor_tensor", lamv2, bc_end, Lv, ALU.subtract)
                yield
                with P.step("DVE", rd=(f"L{c}",), wr=("lam4",)):
                    P.op("tensor_copy", lam4w[:, SLAB:2 * SLAB], Lc)
                yield
                with P.step("ACT", rd=("lam4",), wr=("sa4",)):
                    P.op("activation", sa4w[:], lam4w[:], AF.Exp)
                yield
                with P.step("DVE", rd=("sa4", "k_fm"), wr=("k_fm",)):
                    P.op("tensor_mul", fm(k_fm, c, 0, SLAB),
                         fm(k_fm, c, 0, SLAB), sa4w[:, 0:SLAB])
                yield
                with P.step("DVE", rd=("sa4", "q_fm"), wr=("q_fm",)):
                    P.op("tensor_mul", fm(q_fm, c, 0, SLAB),
                         fm(q_fm, c, 0, SLAB), sa4w[:, SLAB:2 * SLAB])
                yield
            ends = bass.AP(L_fm, CH - 1,
                           [[NC_D * SLAB, 128], [SLAB, NC_D], [CH, NJ]])
            with P.step("ACT", rd=tuple(f"L{c}" for c in range(NC_D)),
                        wr=("scol",)):
                P.op("activation",
                     scol_all[:].rearrange("p (c j) -> p c j", c=NC_D),
                     ends, AF.Exp)
            yield
            for c in range(NC_D):
                for j in () if "p5" in skip else range(NJ):
                    lo = j * CH
                    rj = (c * NJ + j) % 2
                    with P.step("PE", rd=("k_fm", "consts"), wr=("psTS",)):
                        P.transpose(psTS[:, 0:CH], fm(k_fm, c, lo, CH),
                                    ident[:])
                    yield
                    with P.step("DVE", rd=("psTS",), wr=(f"kT{rj}",)):
                        P.op("tensor_copy", kTs[rj][:], psTS[:, 0:CH])
                    yield
                    with P.step("PE", rd=("v_fm", "consts"), wr=("psTS",)):
                        P.transpose(psTS[:, CH:2 * CH], fm(v_fm, c, lo, CH),
                                    ident[:])
                    yield
                    with P.step("DVE", rd=("psTS",), wr=(f"vT{rj}",)):
                        P.op("tensor_copy", vTs[rj][:], psTS[:, CH:2 * CH])
                    yield
                    with P.step("PE", rd=("qa_fm", "ka_fm"), wr=("psAA",)):
                        P.mm(psA0, fm(ka_fm, c, lo, CH)[0:64, :],
                             fm(qa_fm, c, lo, CH)[0:64, :],
                             start=True, stop=True)
                    yield
                    with P.step("PE", rd=("qa_fm", "ka_fm"), wr=("psAA",)):
                        P.mm(psA1, fm(ka_fm, c, lo, CH)[64:128, :],
                             fm(qa_fm, c, lo, CH)[64:128, :],
                             start=True, stop=True)
                    yield
                    with P.step("DVE", rd=("psAA", "consts"),
                                wr=(f"am0_{rj}",)):
                        P.op("select", am0[rj][:], u128[:], psA0, zbf[:],
                             add_drain=True)
                    yield
                    with P.step("DVE", rd=("psAA", "consts"),
                                wr=(f"am1_{rj}",)):
                        P.op("select", am1[rj][:], u128[:], psA1, zbf[:],
                             add_drain=True)
                    yield
                    with P.step("DVE", rd=(f"S{c}",), wr=(f"Sz0_{rj}",)):
                        P.op("tensor_copy", Sz0[rj][0:64, :],
                             S_fp[c][0:64, :])
                    with P.step("DVE", rd=(f"S{c}",), wr=(f"Sz1_{rj}",)):
                        P.op("tensor_copy", Sz1[rj][64:128, :],
                             S_fp[c][64:128, :])
                    yield
                    with P.step("PE", rd=(f"vT{rj}", f"am0_{rj}",
                                          f"am1_{rj}", f"Sz0_{rj}",
                                          f"Sz1_{rj}", "q_fm"),
                                wr=("psAA",)):
                        P.mm(psO[0:64, :], vTs[rj][:, 0:64], am0[rj][:],
                             start=True, stop=False, tile_position=(0, 0))
                        P.mm(psO[0:64, :], Sz0[rj][:], fm(q_fm, c, lo, CH),
                             start=False, stop=True, tile_position=(0, 0))
                        P.mm(psO[64:128, :], vTs[rj][:, 64:128], am1[rj][:],
                             start=True, stop=False, tile_position=(0, 64))
                        P.mm(psO[64:128, :], Sz1[rj][:], fm(q_fm, c, lo, CH),
                             start=False, stop=True, tile_position=(0, 64))
                    yield
                    with P.step("DVE", rd=("psAA",), wr=("y_fm",)):
                        P.op("tensor_copy", fm(y_fm, c, lo, CH), psO)
                    yield
                    with P.step("PE", rd=(f"kT{rj}", f"vT{rj}"),
                                wr=("psAA",)):
                        P.mm(psS[0:64, :], kTs[rj][:, 0:64],
                             vTs[rj][:, 0:64], start=True, stop=True,
                             tile_position=(0, 0))
                        P.mm(psS[64:128, :], kTs[rj][:, 64:128],
                             vTs[rj][:, 64:128], start=True, stop=True,
                             tile_position=(0, 64))
                    yield
                    with P.step("DVE", rd=("psAA", "scol", f"S{c}"),
                                wr=(f"S{c}",)):
                        P.op("scalar_tensor_tensor", S_fp[c][:], S_fp[c][:],
                             scol_all[:, c * NJ + j:c * NJ + j + 1], psS,
                             ALU.mult, ALU.add)
                    yield

        def gen_gates(s):
            sp = s % 2
            XN = (f"xn{sp}h0", f"xn{sp}h1")
            for Ti in range(NT):
                with P.step("PE", rd=("gwr",) + XN, wr=("psY",)):
                    for c in range(NC_D):
                        P.mm(psY[:, 0:NE],
                             fm(xn_fm[sp], c, Ti * 128, 128),
                             gwr[:, c * NE:(c + 1) * NE],
                             start=(c == 0), stop=(c == NC_D - 1))
                yield
                with P.step("DVE", rd=("psY",), wr=("gc0",)):
                    P.op("reduce_max", gc[0][:], psY[:, 0:NE], axis=AX)
                yield
                with P.step("DVE", rd=("gc0",), wr=("gc1",)):
                    P.op("tensor_scalar_mul", gc[1][:], gc[0][:], -1.0)
                yield
                with P.step("ACT", rd=("psY", "gc1"), wr=("gscr",)):
                    P.op("activation", gscr[:], psY[:, 0:NE], AF.Exp,
                         bias=gc[1][:])
                yield
                with P.step("DVE", rd=("gscr",), wr=("gc2",)):
                    P.op("reduce_sum", gc[2][:], gscr[:], axis=AX)
                yield
                with P.step("DVE", rd=("gc2",), wr=("gc3",)):
                    P.op("reciprocal", gc[3][:], gc[2][:])
                yield
                with P.step("DVE", rd=("gscr", "gc3"), wr=("gat8",)):
                    P.op("tensor_scalar_mul", gat8[:], gscr[:], gc[3][:])
                yield
                with P.step("DVE", rd=("gat8", "consts"), wr=("gsb",)):
                    P.op("tensor_add", gsb[:], gat8[:], gbias[:])
                yield
                with P.step("DVE", rd=("gsb",), wr=("gc4",)):
                    P.op("reduce_max", gc[4][:], gsb[:], axis=AX)
                yield
                with P.step("DVE", rd=("gsb", "gc4"), wr=("gmsk",)):
                    P.op("tensor_single_scalar", gmsk[:], gsb[:],
                         gc[4][:], ALU.is_ge)
                yield
                with P.step("DVE", rd=("gmsk", "gsb"), wr=("gm2",)):
                    P.op("scalar_tensor_tensor", gm2[:], gmsk[:],
                         -1e9, gsb[:], ALU.mult, ALU.add)
                yield
                with P.step("DVE", rd=("gm2",), wr=("gc5",)):
                    P.op("reduce_max", gc[5][:], gm2[:], axis=AX)
                yield
                with P.step("DVE", rd=("gsb", "gc5"), wr=("gmsk",)):
                    P.op("tensor_single_scalar", gmsk[:], gsb[:],
                         gc[5][:], ALU.is_ge)
                yield
                with P.step("DVE", rd=("gmsk", "gat8"), wr=(f"gat{Ti}",)):
                    P.op("tensor_mul", gatdb[:, Ti * NE:(Ti + 1) * NE],
                         gmsk[:], gat8[:])
                yield

        def gen_backbone(s):
            sp = s % 2
            XN = (f"xn{sp}h0", f"xn{sp}h1")
            gates = gen_gates(s)

            def produce_h(m, slotA, keysA, mA, slotB, keysB, mB):
                with P.step("PE", rd=keysA + XN, wr=("psP",)):
                    for c in range(NC_D):
                        P.mm(psP[:, 0:SLAB],
                             slotA[:, c * 512 + mA * 128:
                                   c * 512 + (mA + 1) * 128],
                             fm(xn_fm[sp], c, 0, SLAB),
                             start=(c == 0), stop=(c == NC_D - 1))
                yield
                with P.step("PE", rd=keysB + XN, wr=("psQ",)):
                    for c in range(NC_D):
                        P.mm(psQ[:, 0:SLAB],
                             slotB[:, c * 512 + mB * 128:
                                   c * 512 + (mB + 1) * 128],
                             fm(xn_fm[sp], c, 0, SLAB),
                             start=(c == 0), stop=(c == NC_D - 1))
                yield
                with P.step("ACT", rd=("psP",), wr=("hsl",)):
                    P.op("activation", hsl[:], psP[:, 0:SLAB], AF.Silu)
                yield
                with P.step("DVE", rd=("hsl", "psQ"), wr=(f"hsh{m}",)):
                    P.op("tensor_mul", hsh[:, m * SLAB:(m + 1) * SLAB],
                         hsl[:], psQ[:, 0:SLAB])
                yield

            def consume_h(w2slots, mchunks, gate_e):
                nslots = len(w2slots)
                for Ti in range(NT):
                    rdk = tuple(f"hsh{m}" for m in range(mchunks))
                    for _, ks in w2slots:
                        rdk = rdk + ks
                    with P.step("PE", rd=rdk, wr=("psY",)):
                        for m in range(mchunks):
                            gsl, _ = w2slots[m * nslots // mchunks]
                            m_in = m % (mchunks // nslots)
                            for half in range(2):
                                P.mm(psY[:, half * 512:(half + 1) * 512],
                                     hsh[:, m * SLAB + Ti * 128:
                                         m * SLAB + Ti * 128 + 128],
                                     gsl[:, m_in * D + half * 512:
                                         m_in * D + (half + 1) * 512],
                                     start=(m == 0),
                                     stop=(m == mchunks - 1))
                    yield
                    if gate_e is None:
                        with P.step("DVE", rd=("psY",), wr=(f"yacc{Ti}",)):
                            P.op("tensor_copy",
                                 yacc[:, Ti * D:(Ti + 1) * D], psY[:])
                    else:
                        gcol = gatdb[:, Ti * NE + gate_e:
                                     Ti * NE + gate_e + 1]
                        with P.step("DVE", rd=("psY", f"gat{Ti}",
                                               f"yacc{Ti}"),
                                    wr=(f"yacc{Ti}",)):
                            P.op("scalar_tensor_tensor",
                                 yacc[:, Ti * D:(Ti + 1) * D], psY[:],
                                 gcol, yacc[:, Ti * D:(Ti + 1) * D],
                                 ALU.mult, ALU.add)
                    yield

            gates_done = [False]

            def adv_gates(n):
                if gates_done[0]:
                    return
                for _ in range(n):
                    try:
                        next(gates)
                    except StopIteration:
                        gates_done[0] = True
                        return

            for i in range(4):
                sA, kA = load_gran(g_sw1(i))
                sB, kB = load_gran(g_sw3(i))
                yield
                for m4 in range(4):
                    for _ in produce_h(i * 4 + m4, sA, kA, m4, sB, kB, m4):
                        adv_gates(1)
                        yield
            sw2sl = []
            for i in range(4):
                sl, ks = load_gran(g_sw2(i))
                sw2sl.append((sl, ks))
                yield
            for _ in consume_h(sw2sl, NSHC, None):
                adv_gates(2)
                yield
            for e in range(NE):
                esl = []
                for i in range(2):
                    sA, kA = load_gran(g_we(e, 1, i))
                    sB, kB = load_gran(g_we(e, 3, i))
                    yield
                    for m4 in range(4):
                        for _ in produce_h(i * 4 + m4, sA, kA, m4,
                                           sB, kB, m4):
                            adv_gates(1)
                            yield
                for i in range(2):
                    sl, ks = load_gran(g_we(e, 2, i))
                    esl.append((sl, ks))
                    yield
                for _ in consume_h(esl, NIC, e):
                    yield
            adv_gates(100)

        def emit_oproj(s):
            sl0 = (s // (T // SLAB)) * T + (s % (T // SLAB)) * SLAB
            wosl = [load_gran(g_wo(i)) for i in range(2)]
            for Ti in range(NT):
                rdk = ("y_fm",) + wosl[0][1] + wosl[1][1]
                with P.step("PE", rd=rdk, wr=("psY",)):
                    for c in range(NC_D):
                        for half in range(2):
                            P.mm(psY[:, half * 512:(half + 1) * 512],
                                 fm(y_fm, c, Ti * 128, 128),
                                 wosl[half][0][:, c * 512:(c + 1) * 512],
                                 start=(c == 0), stop=(c == NC_D - 1))
                with P.step("POOL", wr=("xc",), q="qX"):
                    P.op("dma_start", out=xc[:],
                         in_=x_d[sl0 + Ti * 128: sl0 + (Ti + 1) * 128, :])
                with P.step("ACT", rd=(f"yacc{Ti}",), wr=("gtmp",)):
                    P.op("activation", gtmp[:],
                         yacc[:, Ti * D:(Ti + 1) * D], AF.Silu)
                with P.step("DVE", rd=("gtmp", "psY"), wr=("gtmp",)):
                    P.op("tensor_mul", gtmp[:], gtmp[:], psY[:])
                with P.step("DVE", rd=("gtmp", "xc"), wr=("gtmp",)):
                    P.op("tensor_add", gtmp[:], gtmp[:], xc[:])
                with P.step("POOL", rd=("gtmp",), q="qO"):
                    P.op("dma_start",
                         out=out_d[sl0 + Ti * 128: sl0 + (Ti + 1) * 128, :],
                         in_=gtmp[:])

        skip = cfg.get("skip", ())

        def _empty():
            return
            yield

        for _ in gen_front(0):
            pass
        for s in range(NSLAB):
            serial = gen_serial(s) if "serial" not in skip else _empty()
            backbone = (gen_backbone(s) if "backbone" not in skip
                        else _empty())
            front = gen_front(s + 1) if s + 1 < NSLAB else None
            s_done = b_done = False
            f_done = front is None
            k = 0
            while not (s_done and b_done):
                if not b_done:
                    try:
                        next(backbone)
                    except StopIteration:
                        b_done = True
                nser = 2 if k % 3 == 0 else 1
                for _ in range(nser):
                    if not s_done:
                        try:
                            next(serial)
                        except StopIteration:
                            s_done = True
                k += 1
                if k % 8 == 0 and not f_done:
                    try:
                        next(front)
                    except StopIteration:
                        f_done = True
            while not f_done:
                try:
                    next(front)
                except StopIteration:
                    f_done = True
            if "oproj" not in skip:
                emit_oproj(s)

        for i in range(QLANES["qO"]):
            lane = f"qO{i}"
            tot = P.cnt[lane]
            P.streams["POOL"].append(
                [lambda h, lane=lane, tot=tot:
                 h.wait_ge(P.sems[lane], tot), None])

        block = es.enter_context(nc.Block())

        @block.tensor
        def _(tensor):
            P.replay("PE", tensor)

        @block.vector
        def _(vector):
            P.replay("DVE", vector)

        @block.scalar
        def _(scalar):
            P.replay("ACT", scalar)

        @block.gpsimd
        def _(gpsimd):
            P.replay("POOL", gpsimd)

        @block.sync
        def _(sync):
            P.replay("SP", sync)

    return nc



def make_consts():
    ident = np.eye(128, dtype=np.float32).astype(bf16)
    u128 = (np.arange(128)[None, :] >= np.arange(128)[:, None]).astype(
        np.uint8)
    zbf = np.zeros((128, 128), np.float32).astype(bf16)
    sel16 = np.zeros((H, D), np.float32)
    for hh in range(H):
        sel16[hh, hh * E:(hh + 1) * E] = 1.0
    epsc = np.full((128, 1), 1e-5, np.float32)
    bgate = np.ones((128, SLAB), np.float32)
    bgate[:, ::CH] = 0.0
    return {"ident": ident, "u128": u128, "zbf": zbf, "sel16": sel16,
            "epsc": epsc, "bgate": bgate}


def pack_d(w):
    ng = w.shape[1] // 512
    return [np.ascontiguousarray(
        w[:, h * 512:(h + 1) * 512].reshape(8, 128, 512)
        .transpose(1, 0, 2).reshape(128, 4096)) for h in range(ng)]


def pack_r(w):
    ng = w.shape[0] // 512
    return [np.ascontiguousarray(
        w[h * 512:(h + 1) * 512, :].reshape(4, 128, 1024)
        .transpose(1, 0, 2).reshape(128, 4096)) for h in range(ng)]


def make_inputs(**inputs):
    ct = np.ascontiguousarray
    f = np.asarray
    grans = [None] * NWALL
    for wi, nm in enumerate(("L0_attn_q_w", "L0_attn_k_w", "L0_attn_v_w")):
        gs = pack_d(f(inputs[nm]).T)
        for hw in range(2):
            grans[g_qkv(wi, hw)] = gs[hw]
    gs1 = pack_d(f(inputs["L0_ffn_sw1"]).T)
    gs3 = pack_d(f(inputs["L0_ffn_sw3"]).T)
    for i in range(4):
        grans[g_sw1(i)] = gs1[i]
        grans[g_sw3(i)] = gs3[i]
    gs2 = pack_r(ct(f(inputs["L0_ffn_sw2"]).T))
    for i in range(4):
        grans[g_sw2(i)] = gs2[i]
    w1 = f(inputs["L0_ffn_w1"])
    w3 = f(inputs["L0_ffn_w3"])
    w2 = f(inputs["L0_ffn_w2"])
    for e in range(NE):
        ge1 = pack_d(w1[e].T)
        ge3 = pack_d(w3[e].T)
        ge2 = pack_r(ct(w2[e].T))
        for i in range(2):
            grans[g_we(e, 1, i)] = ge1[i]
            grans[g_we(e, 3, i)] = ge3[i]
            grans[g_we(e, 2, i)] = ge2[i]
    go = pack_d(f(inputs["L0_attn_o_w"]).T)
    for i in range(2):
        grans[g_wo(i)] = go[i]
    wall = np.stack(grans)

    cs1 = f(inputs["L0_attn_cs_w1"]).T
    cs1p = cs1.reshape(8, 128, 256).transpose(1, 0, 2).reshape(128, 2048)
    cs2 = f(inputs["L0_attn_cs_w2"]).T
    cs2p = cs2.reshape(2, 128, 16).transpose(1, 0, 2).reshape(128, 32)
    gw = f(inputs["L0_ffn_gate_w"]).T
    gwp = gw.reshape(8, 128, 8).transpose(1, 0, 2).reshape(128, 64)
    wsm = ct(np.concatenate([cs1p, cs2p, gwp], axis=1))

    m = {"wall": wall, "wsm": wsm}
    m.update(make_consts())
    m["b1c"] = ct(np.asarray(inputs["L0_attn_cs_b1"],
                             np.float32).reshape(2, 128).T)
    m["b2c"] = ct(np.asarray(inputs["L0_attn_cs_b2"],
                             np.float32).reshape(H, 1))
    m["gbias"] = ct(np.tile(np.asarray(inputs["L0_ffn_gate_bias"],
                                       np.float32)[None, :], (128, 1)))
    m["decf"] = ct(np.asarray(inputs["L0_attn_decay_params"],
                              np.float32).reshape(-1).reshape(NC_D, 128).T)
    return m


def kernel(**inputs):
    NCORES = 8
    x = np.ascontiguousarray(np.asarray(inputs["L0_attn_x"], np.float32))
    Bfull, T, _ = x.shape
    B = Bfull // NCORES
    cfg = {"T": T, "B": B}
    wmap = make_inputs(**inputs)
    in_maps = []
    for c in range(NCORES):
        m = dict(wmap)
        m["x"] = x[c * B:(c + 1) * B].reshape(B * T, D)
        in_maps.append(m)
    nc = build(cfg)
    res = run_bass_kernel_spmd(nc, in_maps, core_ids=list(range(NCORES)))
    out = np.concatenate(
        [res.results[c]["out"].reshape(B, T, D) for c in range(NCORES)],
        axis=0)
    return out


# version 09
# speedup vs baseline: 1.0093x; 1.0093x over previous
import contextlib

import numpy as np
import ml_dtypes

import concourse.bass as bass
import concourse.mybir as mybir
from concourse.bass_utils import run_bass_kernel_spmd

F32 = mybir.dt.float32
BF16 = mybir.dt.bfloat16
U8 = mybir.dt.uint8
AF = mybir.ActivationFunctionType
ALU = mybir.AluOpType
AX = mybir.AxisListType.X
bf16 = ml_dtypes.bfloat16

D = 1024
H = 16
E = 64
DH = 256
NE = 8
INTER = 1024
NSH = 2048
CH = 128
NC_D = D // 128
SLAB = 512
NT = SLAB // 128
NJ = SLAB // CH
NSHC = NSH // 128
NIC = INTER // 128

ACLAMP = 78.0
LN9995 = float(np.log(0.9995))

NWALL = 68


def g_qkv(wi, hw):
    return 2 * wi + hw


def g_sw1(i):
    return 6 + 2 * i


def g_sw3(i):
    return 7 + 2 * i


def g_sw2(i):
    return 14 + i


def g_we(e, kind, i):
    base = 18 + 6 * e
    if kind == 1:
        return base + 2 * i
    if kind == 3:
        return base + 2 * i + 1
    return base + 4 + i


def g_wo(i):
    return 66 + i


ENGS = ("PE", "DVE", "ACT", "POOL", "SP")
QLANES = {"qW": 10, "qX": 3, "qO": 3, "qC": 6, "qT": 4}
QUEUES = tuple(f"{q}{i}" for q, n in QLANES.items() for i in range(n))


class Prog:

    def __init__(self, nc, es, pelag=24):
        self.nc = nc
        self.pelag = pelag
        self.sems = {}
        for name in ENGS + QUEUES:
            self.sems[name] = es.enter_context(nc.semaphore(name))
        self.cnt = {name: 0 for name in ENGS + QUEUES}
        self.qrr = {q: 0 for q in QLANES}
        self.streams = {e: [] for e in ENGS}
        self.last_w = {}
        self.readers = {}
        self.waited = {e: {} for e in ENGS}
        self.mm_total = 0
        self.mm_since_inc = 0
        self.pe_incs = []
        self.pe_wait_idx = 0
        self._cur = None

    def _rec(self, fn):
        self._cur[1].append([fn, None])

    def _mark_inc(self, sem_name, delta):
        assert self._cur[1] and self._cur[1][-1][1] is None
        self.cnt[sem_name] += delta
        self._cur[1][-1][1] = (sem_name, delta)

    def op(self, name, *a, **k):
        self._rec(lambda h: getattr(h, name)(*a, **k))

    def wait(self, sem_name, val):
        eng = self._cur[0]
        if self.waited[eng].get(sem_name, 0) >= val:
            return
        self.waited[eng][sem_name] = val
        s = self.sems[sem_name]
        self._rec(lambda h: h.wait_ge(s, val))

    def mm(self, *a, **k):
        assert self._cur[0] == "PE"
        kind = k.pop("kind", "matmul")
        w = 4 if kind == "transpose" else 1
        while (self.pe_wait_idx < len(self.pe_incs)
               and self.pe_incs[self.pe_wait_idx][1]
               <= self.mm_total - self.pelag):
            self.pe_wait_idx += 1
        if self.pe_wait_idx > 0:
            self.wait("PE", self.pe_incs[self.pe_wait_idx - 1][0])
        self._rec(lambda h: getattr(h, kind)(*a, **k))
        self.mm_total += w
        self.mm_since_inc += w
        if self.mm_since_inc >= 8:
            self._mark_inc("PE", 1)
            self.pe_incs.append((self.cnt["PE"], self.mm_total))
            self.mm_since_inc = 0

    def transpose(self, out, in_, ident):
        self.mm(out, in_, ident, kind="transpose")

    def _deps(self, rd, wr):
        need = {}

        def add(t):
            if t:
                need[t[0]] = max(need.get(t[0], 0), t[1])

        for r in rd:
            add(self.last_w.get(r))
        for r in wr:
            add(self.last_w.get(r))
            for t in self.readers.get(r, ()):
                add(t)
        return need

    @contextlib.contextmanager
    def step(self, eng, rd=(), wr=(), q=None):
        assert self._cur is None
        ops = []
        self._cur = (eng, ops)
        if q:
            lane = f"{q}{self.qrr[q]}"
            self.qrr[q] = (self.qrr[q] + 1) % QLANES[q]
            self.wait(lane, self.cnt[lane])
            sem = lane
        else:
            sem = eng
        for s, v in sorted(self._deps(rd, wr).items()):
            self.wait(s, v)
        yield self
        if ops and ops[-1][1] is None:
            self._mark_inc(sem, 16 if q else 1)
            if eng == "PE" and not q and self.mm_since_inc:
                self.pe_incs.append((self.cnt["PE"], self.mm_total))
                self.mm_since_inc = 0
        token = (sem, self.cnt[sem])
        for r in rd:
            self.readers.setdefault(r, []).append(token)
        for r in wr:
            self.last_w[r] = token
            self.readers[r] = []
        self.streams[eng].extend(ops)
        self._cur = None

    def replay(self, eng, h):
        for fn, inc in self.streams[eng]:
            ins = fn(h)
            if inc is not None:
                ins.then_inc(self.sems[inc[0]], inc[1])


def build(cfg):
    T = cfg["T"]
    B = cfg["B"]
    assert T % SLAB == 0
    NSLAB = T // SLAB * B
    TOK = B * T
    NRING = cfg.get("nring", 5)

    nc = bass.Bass("TRN2", target_bir_lowering=False)

    def din(name, shape, dt=F32):
        return nc.dram_tensor(name, shape, dt, kind="ExternalInput").ap()

    x_d = din("x", [TOK, D])
    wall_d = din("wall", [NWALL, 128, 4096])
    wsm_d = din("wsm", [128, 2048 + 32 + 64])
    c_d = {nm: din(nm, sh, dt) for nm, sh, dt in [
        ("tident", [128, 128], BF16), ("u128", [128, 128], U8),
        ("zbf", [128, 128], BF16), ("sel16", [H, D], F32),
        ("b1c", [128, 2], F32), ("b2c", [H, 1], F32),
        ("gbias", [128, NE], F32), ("epsc", [128, 1], F32),
        ("decf", [128, NC_D], F32), ("bgate", [128, SLAB], F32)]}
    out_d = nc.dram_tensor("out", [TOK, D], F32, kind="ExternalOutput").ap()

    wallb = nc.dram_tensor("wallb", [NWALL, 128, 4096], BF16,
                           kind="Internal").ap()
    wsmb = nc.dram_tensor("wsmb", [128, 2048 + 32 + 64], BF16,
                          kind="Internal").ap()

    es = contextlib.ExitStack()
    with es:
        def sb(name, shape, dt=F32):
            return es.enter_context(nc.sbuf_tensor(name, shape, dt))

        ring = [sb(f"ring{i}", [128, 4096], BF16) for i in range(NRING)]
        xn_fm = [sb(f"xnfm{i}", [128, NC_D * SLAB], BF16) for i in range(2)]
        q_fm = sb("q_fm", [128, NC_D * SLAB], BF16)
        k_fm = sb("k_fm", [128, NC_D * SLAB], BF16)
        v_fm = sb("v_fm", [128, NC_D * SLAB], BF16)
        qa_fm = sb("qa_fm", [128, NC_D * SLAB], BF16)
        ka_fm = sb("ka_fm", [128, NC_D * SLAB], BF16)
        scol_all = sb("scol_all", [128, NC_D * NJ])
        L_fm = sb("L_fm", [128, NC_D * SLAB])
        y_fm = sb("y_fm", [128, NC_D * SLAB], BF16)
        hsh = sb("hsh", [128, NSHC * SLAB], BF16)
        yacc = sb("yacc", [128, NT * D])
        xa = [sb(f"xa{i}", [128, D]) for i in range(2)]
        xc = sb("xc", [128, D])
        sqs = sb("sqs", [128, 512])
        xn_tm = [sb(f"xntm{i}", [128, D], BF16) for i in range(2)]
        z1t = sb("z1t", [128, 2 * SLAB], BF16)
        t2t = sb("t2t", [H, SLAB])
        lamw3 = sb("lamw3", [128, SLAB])
        lam4w = sb("lam4w", [128, 2 * SLAB])
        sa4w = sb("sa4w", [128, 2 * SLAB], BF16)
        col = [[sb(f"col{p}_{i}", [128, 1]) for i in range(8)]
               for p in range(2)]
        kTs = [sb(f"kT{i}", [128, CH], BF16) for i in range(4)]
        vTs = [sb(f"vT{i}", [128, CH], BF16) for i in range(4)]
        am0 = [sb(f"am0_{i}", [128, CH], BF16) for i in range(4)]
        am1 = [sb(f"am1_{i}", [128, CH], BF16) for i in range(4)]
        Sz0 = [sb(f"Sz0_{i}", [128, E], BF16) for i in range(4)]
        Sz1 = [sb(f"Sz1_{i}", [128, E], BF16) for i in range(4)]
        S_fp = [sb(f"S_fp{c}", [128, E]) for c in range(NC_D)]
        hsl2 = [sb(f"hsl{i}", [128, SLAB], BF16) for i in range(2)]
        gtmp = sb("gtmp", [128, D])
        gscr = sb("gscr", [128, NE])
        gat8 = sb("gat8", [128, NE])
        gsb = sb("gsb", [128, NE])
        gmsk = sb("gmsk", [128, NE])
        gm2 = sb("gm2", [128, NE])
        gatdb = sb("gatdb", [128, NT * NE])
        gc = [sb(f"gc{i}", [128, 1]) for i in range(6)]
        cs1r = sb("cs1r", [128, 2048], BF16)
        cs2r = sb("cs2r", [128, 32], BF16)
        gwr = sb("gwr", [128, 64], BF16)
        tident = sb("tidentc", [128, 128], BF16)
        u128 = sb("u128c", [128, 128], U8)
        zbf = sb("zbfc", [128, 128], BF16)
        sel16 = sb("sel16c", [H, D])
        b1c = sb("b1cc", [128, 2])
        b2c = sb("b2cc", [H, 1])
        gbias = sb("gbiasc", [128, NE])
        epsc = sb("epscc", [128, 1])
        decf = sb("decfc", [128, NC_D])
        bgate = sb("bgatec", [128, SLAB])
        basec = sb("basec", [128, NC_D])
        s1c = sb("s1c", [128, NC_D])
        s2c = sb("s2c", [128, NC_D])
        b2h = sb("b2h", [H, 1])

        psP = es.enter_context(nc.psum_tensor("psP", [128, 512], F32))
        psQ = es.enter_context(nc.psum_tensor("psQ", [128, 512], F32))
        psY = es.enter_context(nc.psum_tensor("psY", [128, 1024], F32))
        psL = es.enter_context(nc.psum_tensor("psL", [128, 512], F32))
        psTS = es.enter_context(nc.psum_tensor("psTS", [128, 512], BF16))
        psTB = es.enter_context(nc.psum_tensor("psTB", [128, 512], BF16))
        psAA = es.enter_context(nc.psum_tensor("psAA", [128, 512], F32))
        psA0 = psAA[:, 0:128]
        psA1 = psAA[:, 128:256]
        psO = psAA[:, 256:384]
        psS = psAA[:, 384:448]

        P = Prog(nc, es)

        items = [("tident", tident), ("u128", u128), ("zbf", zbf),
                 ("sel16", sel16), ("b1c", b1c), ("b2c", b2c),
                 ("gbias", gbias), ("epsc", epsc), ("decf", decf),
                 ("bgate", bgate)]
        for nm, tile in items:
            with P.step("SP", wr=(f"const_{nm}",), q="qW"):
                P.op("dma_start", out=tile[:], in_=c_d[nm])
        with P.step("SP", rd=tuple(f"const_{nm}" for nm, _ in items),
                    wr=("consts",)):
            P.op("nop")

        with P.step("ACT", rd=("consts",), wr=("basec",)):
            P.op("activation", basec[:], decf[:], AF.Sigmoid)
        with P.step("DVE", rd=("basec",), wr=("s1c",)):
            P.op("tensor_scalar_mul", s1c[:], basec[:], 0.1)
        with P.step("DVE", rd=("basec",), wr=("s2c",)):
            P.op("tensor_scalar_mul", s2c[:], basec[:], 1.1)
        with P.step("DVE", rd=("consts",), wr=("b2h",)):
            P.op("tensor_scalar_mul", b2h[:], b2c[:], 0.5)
        for r in range(4):
            with P.step("DVE", wr=(f"Sz0_{r}",)):
                P.op("memset", Sz0[r][:], 0.0)
            with P.step("DVE", wr=(f"Sz1_{r}",)):
                P.op("memset", Sz1[r][:], 0.0)

        with P.step("POOL", wr=("cgs",), q="qC"):
            P.op("dma_start", out=wsmb, in_=wsm_d)
        for g in range(NWALL):
            with P.step("POOL", wr=(f"cg{g}",), q="qC"):
                P.op("dma_start", out=wallb[g], in_=wall_d[g])

        with P.step("SP", rd=("cgs",), wr=("cs1r",), q="qW"):
            P.op("dma_start", out=cs1r[:], in_=wsmb[:, 0:2048])
        with P.step("SP", rd=("cgs",), wr=("cs2r",), q="qW"):
            P.op("dma_start", out=cs2r[:], in_=wsmb[:, 2048:2080])
        with P.step("SP", rd=("cgs",), wr=("gwr",), q="qW"):
            P.op("dma_start", out=gwr[:], in_=wsmb[:, 2080:2144])

        ring_i = [0]

        def load_gran(g):
            si = ring_i[0] % NRING
            ring_i[0] += 1
            slot = ring[si]
            keys = (f"rg{si}a", f"rg{si}b")
            for h in range(2):
                with P.step("SP", rd=(f"cg{g}",), wr=(keys[h],), q="qW"):
                    P.op("dma_start", out=slot[h * 64:(h + 1) * 64, :],
                         in_=wallb[g][h * 64:(h + 1) * 64, :])
            return slot, keys

        def fm(tile, c, lo, n):
            return tile[:, c * SLAB + lo: c * SLAB + lo + n]

        def evac(rd, wr, dst_ap, src_ap):
            with P.step("DVE", rd=rd, wr=wr):
                P.op("tensor_copy", dst_ap, src_ap)


        def gen_front(s):
            sp = s % 2
            sl0 = (s // (T // SLAB)) * T + (s % (T // SLAB)) * SLAB
            XW = (f"xn{sp}h0", f"xn{sp}h1")
            for Ti in range(NT):
                p = Ti % 2
                xt = xa[p]
                xkey = f"xa{p}"
                cl = col[p]
                ck = [f"c{p}_{i}" for i in range(8)]
                with P.step("POOL", wr=(xkey,), q="qX"):
                    P.op("dma_start", out=xt[:],
                         in_=x_d[sl0 + Ti * 128: sl0 + (Ti + 1) * 128, :])
                yield
                with P.step("DVE", rd=(xkey,), wr=(ck[0],)):
                    P.op("reduce_sum", cl[0][:], xt[:], axis=AX)
                yield
                with P.step("DVE", rd=(xkey,), wr=("sqs",)):
                    P.op("tensor_mul", sqs[:], xt[:, 0:512], xt[:, 0:512])
                with P.step("DVE", rd=("sqs",), wr=(ck[6],)):
                    P.op("reduce_sum", cl[6][:], sqs[:], axis=AX)
                yield
                with P.step("DVE", rd=(xkey, ck[6]), wr=("sqs",)):
                    P.op("tensor_mul", sqs[:], xt[:, 512:1024],
                         xt[:, 512:1024])
                with P.step("DVE", rd=("sqs",), wr=(ck[7],)):
                    P.op("reduce_sum", cl[7][:], sqs[:], axis=AX)
                yield
                with P.step("DVE", rd=(ck[6], ck[7]), wr=(ck[1],)):
                    P.op("tensor_add", cl[1][:], cl[6][:], cl[7][:])
                yield
                with P.step("DVE", rd=(ck[0],), wr=(ck[2],)):
                    P.op("tensor_scalar_mul", cl[2][:], cl[0][:], 1.0 / D)
                with P.step("DVE", rd=(ck[1],), wr=(ck[3],)):
                    P.op("tensor_scalar_mul", cl[3][:], cl[1][:], 1.0 / D)
                with P.step("DVE", rd=(ck[2],), wr=(ck[4],)):
                    P.op("tensor_mul", cl[4][:], cl[2][:], cl[2][:])
                with P.step("DVE", rd=(ck[3], ck[4]), wr=(ck[5],)):
                    P.op("tensor_sub", cl[5][:], cl[3][:], cl[4][:])
                yield
                with P.step("ACT", rd=(ck[5], "consts"), wr=(ck[6],)):
                    P.op("activation", cl[6][:], cl[5][:], AF.Ln,
                         bias=epsc[:])
                with P.step("ACT", rd=(ck[6],), wr=(ck[7],)):
                    P.op("activation", cl[7][:], cl[6][:], AF.Exp,
                         scale=-0.5)
                yield
                with P.step("DVE", rd=(xkey, ck[2], ck[7]),
                            wr=(f"xntm{p}",)):
                    P.op("tensor_scalar", xn_tm[p][:], xt[:], cl[2][:],
                         cl[7][:], ALU.subtract, ALU.mult)
                yield
                for c in range(NC_D):
                    with P.step("SP", rd=(f"xntm{p}",),
                                wr=(XW[c // 4],), q="qT"):
                        P.op("dma_start",
                             out=xn_fm[sp][:, c * SLAB + Ti * 128:
                                           c * SLAB + Ti * 128 + 128],
                             in_=xn_tm[p][:, c * 128:(c + 1) * 128],
                             transpose=True)
                    if c % 4 == 3:
                        yield

        def gen_serial(s):
            skip = cfg.get("skip", ())
            sp = s % 2
            XN = (f"xn{sp}h0", f"xn{sp}h1")
            if s % (T // SLAB) == 0:
                for c in range(NC_D):
                    with P.step("DVE", wr=(f"S{c}",)):
                        P.op("memset", S_fp[c][:], 0.0)
                yield
            for wi, (dsttile, dkey) in () if "p2" in skip else enumerate(
                    ((q_fm, "q_fm"), (k_fm, "k_fm"), (v_fm, "v_fm"))):
                for hw in range(2):
                    slot, keys = load_gran(g_qkv(wi, hw))
                    yield
                    for cp4 in range(4):
                        cp = hw * 4 + cp4
                        with P.step("PE", rd=keys + XN, wr=("psL",)):
                            for c in range(NC_D):
                                P.mm(psL[:, 0:SLAB],
                                     slot[:, c * 512 + cp4 * 128:
                                          c * 512 + (cp4 + 1) * 128],
                                     fm(xn_fm[sp], c, 0, SLAB),
                                     start=(c == 0), stop=(c == NC_D - 1))
                        yield
                        evac(("psL",), (dkey,), fm(dsttile, cp, 0, SLAB),
                             psL[:, 0:SLAB])
                        yield
            if "p3" in skip:
                return
            for mp in range(2):
                with P.step("PE", rd=("cs1r",) + XN, wr=("psL",)):
                    for c in range(NC_D):
                        P.mm(psL[:, 0:SLAB],
                             cs1r[:, c * 256 + mp * 128:
                                  c * 256 + (mp + 1) * 128],
                             fm(xn_fm[sp], c, 0, SLAB),
                             start=(c == 0), stop=(c == NC_D - 1))
                yield
                with P.step("ACT", rd=("psL", "consts"), wr=(f"z1t{mp}",)):
                    P.op("activation", z1t[:, mp * SLAB:(mp + 1) * SLAB],
                         psL[:, 0:SLAB], AF.Tanh, bias=b1c[:, mp:mp + 1])
                yield
            with P.step("PE", rd=("cs2r", "z1t0", "z1t1"), wr=("psL",)):
                for c2 in range(2):
                    P.mm(psL[0:H, 0:SLAB], cs2r[:, c2 * H:(c2 + 1) * H],
                         z1t[:, c2 * SLAB:(c2 + 1) * SLAB],
                         start=(c2 == 0), stop=(c2 == 1))
            yield
            with P.step("ACT", rd=("psL", "b2h"), wr=("t2t",)):
                P.op("activation", t2t[:], psL[0:H, 0:SLAB], AF.Tanh,
                     bias=b2h[:], scale=0.5)
            yield
            for c in range(NC_D):
                with P.step("PE", rd=("t2t", "consts"), wr=("psL",)):
                    P.mm(psL[:, 0:SLAB], sel16[:, c * 128:(c + 1) * 128],
                         t2t[:], start=True, stop=True)
                yield
                with P.step("DVE", rd=("psL", "s1c", "s2c"), wr=("lamw3",)):
                    P.op("tensor_scalar", lamw3[:], psL[:, 0:SLAB],
                         s1c[:, c:c + 1], s2c[:, c:c + 1],
                         ALU.mult, ALU.add)
                yield
                with P.step("ACT", rd=("lamw3",), wr=("lamw3",)):
                    P.op("activation", lamw3[:], lamw3[:], AF.Ln)
                yield
                with P.step("DVE", rd=("lamw3",), wr=("lamw3",)):
                    P.op("tensor_scalar_min", lamw3[:], lamw3[:], LN9995)
                yield
                with P.step("DVE", rd=("lamw3", "consts"), wr=(f"L{c}",)):
                    P.op("tensor_tensor_scan", fm(L_fm, c, 0, SLAB),
                         bgate[:], lamw3[:], 0.0, ALU.mult, ALU.add)
                yield
            if "p45" in skip:
                return
            for c in range(NC_D):
                Lc = fm(L_fm, c, 0, SLAB)
                Lv = Lc.rearrange("p (j u) -> p j u", j=NJ)
                bc_mid = bass.AP(L_fm, c * SLAB + CH // 2 - 1,
                                 [[NC_D * SLAB, 128], [CH, NJ], [0, CH]])
                bc_end = bass.AP(L_fm, c * SLAB + CH - 1,
                                 [[NC_D * SLAB, 128], [CH, NJ], [0, CH]])
                lamv = lam4w[:, 0:SLAB].rearrange("p (j u) -> p j u", j=NJ)
                with P.step("DVE", rd=(f"L{c}",), wr=("lam4",)):
                    P.op("tensor_tensor", lamv, Lv, bc_mid, ALU.subtract)
                yield
                with P.step("DVE", rd=("lam4",), wr=("lam4",)):
                    P.op("tensor_scalar", lam4w[:, 0:SLAB], lam4w[:, 0:SLAB],
                         ACLAMP, -ACLAMP, ALU.min, ALU.max)
                yield
                with P.step("DVE", rd=("lam4",), wr=("lam4",)):
                    P.op("tensor_scalar_mul", lam4w[:, SLAB:2 * SLAB],
                         lam4w[:, 0:SLAB], -1.0)
                yield
                with P.step("ACT", rd=("lam4",), wr=("sa4",)):
                    P.op("activation", sa4w[:], lam4w[:], AF.Exp)
                yield
                with P.step("DVE", rd=("sa4", "q_fm"), wr=("qa_fm",)):
                    P.op("tensor_mul", fm(qa_fm, c, 0, SLAB),
                         fm(q_fm, c, 0, SLAB), sa4w[:, 0:SLAB])
                yield
                with P.step("DVE", rd=("sa4", "k_fm"), wr=("ka_fm",)):
                    P.op("tensor_mul", fm(ka_fm, c, 0, SLAB),
                         fm(k_fm, c, 0, SLAB), sa4w[:, SLAB:2 * SLAB])
                yield
                lamv2 = lam4w[:, 0:SLAB].rearrange("p (j u) -> p j u", j=NJ)
                with P.step("DVE", rd=(f"L{c}",), wr=("lam4",)):
                    P.op("tensor_tensor", lamv2, bc_end, Lv, ALU.subtract)
                yield
                with P.step("DVE", rd=(f"L{c}",), wr=("lam4",)):
                    P.op("tensor_copy", lam4w[:, SLAB:2 * SLAB], Lc)
                yield
                with P.step("ACT", rd=("lam4",), wr=("sa4",)):
                    P.op("activation", sa4w[:], lam4w[:], AF.Exp)
                yield
                with P.step("DVE", rd=("sa4", "k_fm"), wr=("k_fm",)):
                    P.op("tensor_mul", fm(k_fm, c, 0, SLAB),
                         fm(k_fm, c, 0, SLAB), sa4w[:, 0:SLAB])
                yield
                with P.step("DVE", rd=("sa4", "q_fm"), wr=("q_fm",)):
                    P.op("tensor_mul", fm(q_fm, c, 0, SLAB),
                         fm(q_fm, c, 0, SLAB), sa4w[:, SLAB:2 * SLAB])
                yield
            ends = bass.AP(L_fm, CH - 1,
                           [[NC_D * SLAB, 128], [SLAB, NC_D], [CH, NJ]])
            with P.step("ACT", rd=tuple(f"L{c}" for c in range(NC_D)),
                        wr=("scol",)):
                P.op("activation",
                     scol_all[:].rearrange("p (c j) -> p c j", c=NC_D),
                     ends, AF.Exp)
            yield
            def scan_chunk(c, ci, pA0, pA1, pO, pS, bk, pT, ptk):
                for j in () if "p5" in skip else range(NJ):
                    lo = j * CH
                    rb = 2 * ci + (j % 2)
                    with P.step("PE", rd=("k_fm", "consts"), wr=(ptk,)):
                        P.transpose(pT[:, 0:CH], fm(k_fm, c, lo, CH),
                                    tident[:])
                    yield
                    with P.step("DVE", rd=(ptk,), wr=(f"kT{rb}",)):
                        P.op("tensor_copy", kTs[rb][:], pT[:, 0:CH])
                    yield
                    with P.step("PE", rd=("v_fm", "consts"), wr=(ptk,)):
                        P.transpose(pT[:, CH:2 * CH], fm(v_fm, c, lo, CH),
                                    tident[:])
                    yield
                    with P.step("DVE", rd=(ptk,), wr=(f"vT{rb}",)):
                        P.op("tensor_copy", vTs[rb][:], pT[:, CH:2 * CH])
                    yield
                    with P.step("PE", rd=("qa_fm", "ka_fm"), wr=(bk,)):
                        P.mm(pA0, fm(ka_fm, c, lo, CH)[0:64, :],
                             fm(qa_fm, c, lo, CH)[0:64, :],
                             start=True, stop=True)
                    yield
                    with P.step("PE", rd=("qa_fm", "ka_fm"), wr=(bk,)):
                        P.mm(pA1, fm(ka_fm, c, lo, CH)[64:128, :],
                             fm(qa_fm, c, lo, CH)[64:128, :],
                             start=True, stop=True)
                    yield
                    with P.step("DVE", rd=(bk, "consts"),
                                wr=(f"am0_{rb}",)):
                        P.op("select", am0[rb][:], u128[:], pA0, zbf[:],
                             add_drain=True)
                    yield
                    with P.step("DVE", rd=(bk, "consts"),
                                wr=(f"am1_{rb}",)):
                        P.op("select", am1[rb][:], u128[:], pA1, zbf[:],
                             add_drain=True)
                    yield
                    with P.step("DVE", rd=(f"S{c}",), wr=(f"Sz0_{rb}",)):
                        P.op("tensor_copy", Sz0[rb][0:64, :],
                             S_fp[c][0:64, :])
                    with P.step("DVE", rd=(f"S{c}",), wr=(f"Sz1_{rb}",)):
                        P.op("tensor_copy", Sz1[rb][64:128, :],
                             S_fp[c][64:128, :])
                    yield
                    with P.step("PE", rd=(f"vT{rb}", f"am0_{rb}",
                                          f"am1_{rb}", f"Sz0_{rb}",
                                          f"Sz1_{rb}", "q_fm"), wr=(bk,)):
                        P.mm(pO[0:64, :], vTs[rb][:, 0:64], am0[rb][:],
                             start=True, stop=False, tile_position=(0, 0))
                        P.mm(pO[0:64, :], Sz0[rb][:], fm(q_fm, c, lo, CH),
                             start=False, stop=True, tile_position=(0, 0))
                        P.mm(pO[64:128, :], vTs[rb][:, 64:128],
                             am1[rb][:], start=True, stop=False,
                             tile_position=(0, 64))
                        P.mm(pO[64:128, :], Sz1[rb][:],
                             fm(q_fm, c, lo, CH), start=False, stop=True,
                             tile_position=(0, 64))
                    yield
                    with P.step("DVE", rd=(bk,), wr=("y_fm",)):
                        P.op("tensor_copy", fm(y_fm, c, lo, CH), pO)
                    yield
                    with P.step("PE", rd=(f"kT{rb}", f"vT{rb}"), wr=(bk,)):
                        P.mm(pS[0:64, :], kTs[rb][:, 0:64],
                             vTs[rb][:, 0:64], start=True, stop=True,
                             tile_position=(0, 0))
                        P.mm(pS[64:128, :], kTs[rb][:, 64:128],
                             vTs[rb][:, 64:128], start=True, stop=True,
                             tile_position=(0, 64))
                    yield
                    with P.step("DVE", rd=(bk, "scol", f"S{c}"),
                                wr=(f"S{c}",)):
                        P.op("scalar_tensor_tensor", S_fp[c][:],
                             S_fp[c][:],
                             scol_all[:, c * NJ + j:c * NJ + j + 1], pS,
                             ALU.mult, ALU.add)
                    yield

            for cp in range(NC_D // 2):
                gA = scan_chunk(2 * cp, 0, psA0, psA1, psO, psS,
                                "psAA", psTS, "psTS")
                if cfg.get("single_chain"):
                    alive = [gA]
                else:
                    gB = scan_chunk(2 * cp + 1, 1, psL[:, 0:128],
                                    psL[:, 128:256], psL[:, 256:384],
                                    psL[:, 384:448], "psL", psTB, "psTB")
                    alive = [gA, gB]
                while alive:
                    for g in list(alive):
                        try:
                            next(g)
                        except StopIteration:
                            alive.remove(g)
                            continue
                        yield

        def gen_gates(s):
            sp = s % 2
            XN = (f"xn{sp}h0", f"xn{sp}h1")
            for Ti in range(NT):
                with P.step("PE", rd=("gwr",) + XN, wr=("psY",)):
                    for c in range(NC_D):
                        P.mm(psY[:, 0:NE],
                             fm(xn_fm[sp], c, Ti * 128, 128),
                             gwr[:, c * NE:(c + 1) * NE],
                             start=(c == 0), stop=(c == NC_D - 1))
                yield
                with P.step("DVE", rd=("psY",), wr=("gc0",)):
                    P.op("reduce_max", gc[0][:], psY[:, 0:NE], axis=AX)
                yield
                with P.step("DVE", rd=("gc0",), wr=("gc1",)):
                    P.op("tensor_scalar_mul", gc[1][:], gc[0][:], -1.0)
                yield
                with P.step("ACT", rd=("psY", "gc1"), wr=("gscr",)):
                    P.op("activation", gscr[:], psY[:, 0:NE], AF.Exp,
                         bias=gc[1][:])
                yield
                with P.step("DVE", rd=("gscr",), wr=("gc2",)):
                    P.op("reduce_sum", gc[2][:], gscr[:], axis=AX)
                yield
                with P.step("DVE", rd=("gc2",), wr=("gc3",)):
                    P.op("reciprocal", gc[3][:], gc[2][:])
                yield
                with P.step("DVE", rd=("gscr", "gc3"), wr=("gat8",)):
                    P.op("tensor_scalar_mul", gat8[:], gscr[:], gc[3][:])
                yield
                with P.step("DVE", rd=("gat8", "consts"), wr=("gsb",)):
                    P.op("tensor_add", gsb[:], gat8[:], gbias[:])
                yield
                with P.step("DVE", rd=("gsb",), wr=("gc4",)):
                    P.op("reduce_max", gc[4][:], gsb[:], axis=AX)
                yield
                with P.step("DVE", rd=("gsb", "gc4"), wr=("gmsk",)):
                    P.op("tensor_single_scalar", gmsk[:], gsb[:],
                         gc[4][:], ALU.is_ge)
                yield
                with P.step("DVE", rd=("gmsk", "gsb"), wr=("gm2",)):
                    P.op("scalar_tensor_tensor", gm2[:], gmsk[:],
                         -1e9, gsb[:], ALU.mult, ALU.add)
                yield
                with P.step("DVE", rd=("gm2",), wr=("gc5",)):
                    P.op("reduce_max", gc[5][:], gm2[:], axis=AX)
                yield
                with P.step("DVE", rd=("gsb", "gc5"), wr=("gmsk",)):
                    P.op("tensor_single_scalar", gmsk[:], gsb[:],
                         gc[5][:], ALU.is_ge)
                yield
                with P.step("DVE", rd=("gmsk", "gat8"), wr=(f"gat{Ti}",)):
                    P.op("tensor_mul", gatdb[:, Ti * NE:(Ti + 1) * NE],
                         gmsk[:], gat8[:])
                yield

        def gen_backbone(s):
            sp = s % 2
            XN = (f"xn{sp}h0", f"xn{sp}h1")
            gates = gen_gates(s)

            def produce_h(m, slotA, keysA, mA, slotB, keysB, mB):
                pp, ppk, qq, qqk = (psP, "psP", psQ, "psQ")
                hb = hsl2[m % 2]
                hbk = f"hsl{m % 2}"
                with P.step("PE", rd=keysA + XN, wr=(ppk,)):
                    for c in range(NC_D):
                        P.mm(pp[:, 0:SLAB],
                             slotA[:, c * 512 + mA * 128:
                                   c * 512 + (mA + 1) * 128],
                             fm(xn_fm[sp], c, 0, SLAB),
                             start=(c == 0), stop=(c == NC_D - 1))
                yield
                with P.step("PE", rd=keysB + XN, wr=(qqk,)):
                    for c in range(NC_D):
                        P.mm(qq[:, 0:SLAB],
                             slotB[:, c * 512 + mB * 128:
                                   c * 512 + (mB + 1) * 128],
                             fm(xn_fm[sp], c, 0, SLAB),
                             start=(c == 0), stop=(c == NC_D - 1))
                yield
                with P.step("ACT", rd=(ppk,), wr=(hbk,)):
                    P.op("activation", hb[:], pp[:, 0:SLAB], AF.Silu)
                yield
                with P.step("DVE", rd=(hbk, qqk), wr=(f"hsh{m}",)):
                    P.op("tensor_mul", hsh[:, m * SLAB:(m + 1) * SLAB],
                         hb[:], qq[:, 0:SLAB])
                yield

            def consume_h(w2slots, mchunks, gate_e):
                nslots = len(w2slots)
                for Ti in range(NT):
                    rdk = tuple(f"hsh{m}" for m in range(mchunks))
                    for _, ks in w2slots:
                        rdk = rdk + ks
                    with P.step("PE", rd=rdk, wr=("psY",)):
                        for m in range(mchunks):
                            gsl, _ = w2slots[m * nslots // mchunks]
                            m_in = m % (mchunks // nslots)
                            for half in range(2):
                                P.mm(psY[:, half * 512:(half + 1) * 512],
                                     hsh[:, m * SLAB + Ti * 128:
                                         m * SLAB + Ti * 128 + 128],
                                     gsl[:, m_in * D + half * 512:
                                         m_in * D + (half + 1) * 512],
                                     start=(m == 0),
                                     stop=(m == mchunks - 1))
                    yield
                    if gate_e is None:
                        with P.step("DVE", rd=("psY",), wr=(f"yacc{Ti}",)):
                            P.op("tensor_copy",
                                 yacc[:, Ti * D:(Ti + 1) * D], psY[:])
                    else:
                        gcol = gatdb[:, Ti * NE + gate_e:
                                     Ti * NE + gate_e + 1]
                        with P.step("DVE", rd=("psY", f"gat{Ti}",
                                               f"yacc{Ti}"),
                                    wr=(f"yacc{Ti}",)):
                            P.op("scalar_tensor_tensor",
                                 yacc[:, Ti * D:(Ti + 1) * D], psY[:],
                                 gcol, yacc[:, Ti * D:(Ti + 1) * D],
                                 ALU.mult, ALU.add)
                    yield

            gates_done = [False]

            def adv_gates(n):
                if gates_done[0]:
                    return
                for _ in range(n):
                    try:
                        next(gates)
                    except StopIteration:
                        gates_done[0] = True
                        return

            for i in range(4):
                sA, kA = load_gran(g_sw1(i))
                sB, kB = load_gran(g_sw3(i))
                yield
                for m4 in range(4):
                    for _ in produce_h(i * 4 + m4, sA, kA, m4, sB, kB, m4):
                        adv_gates(1)
                        yield
            sw2sl = []
            for i in range(4):
                sl, ks = load_gran(g_sw2(i))
                sw2sl.append((sl, ks))
                yield
            for _ in consume_h(sw2sl, NSHC, None):
                adv_gates(2)
                yield
            for e in range(NE):
                esl = []
                for i in range(2):
                    sA, kA = load_gran(g_we(e, 1, i))
                    sB, kB = load_gran(g_we(e, 3, i))
                    yield
                    for m4 in range(4):
                        for _ in produce_h(i * 4 + m4, sA, kA, m4,
                                           sB, kB, m4):
                            adv_gates(1)
                            yield
                for i in range(2):
                    sl, ks = load_gran(g_we(e, 2, i))
                    esl.append((sl, ks))
                    yield
                for _ in consume_h(esl, NIC, e):
                    yield
            adv_gates(100)

        def emit_oproj(s):
            sl0 = (s // (T // SLAB)) * T + (s % (T // SLAB)) * SLAB
            wosl = [load_gran(g_wo(i)) for i in range(2)]
            for Ti in range(NT):
                rdk = ("y_fm",) + wosl[0][1] + wosl[1][1]
                with P.step("PE", rd=rdk, wr=("psY",)):
                    for c in range(NC_D):
                        for half in range(2):
                            P.mm(psY[:, half * 512:(half + 1) * 512],
                                 fm(y_fm, c, Ti * 128, 128),
                                 wosl[half][0][:, c * 512:(c + 1) * 512],
                                 start=(c == 0), stop=(c == NC_D - 1))
                with P.step("POOL", wr=("xc",), q="qX"):
                    P.op("dma_start", out=xc[:],
                         in_=x_d[sl0 + Ti * 128: sl0 + (Ti + 1) * 128, :])
                with P.step("ACT", rd=(f"yacc{Ti}",), wr=("gtmp",)):
                    P.op("activation", gtmp[:],
                         yacc[:, Ti * D:(Ti + 1) * D], AF.Silu)
                with P.step("DVE", rd=("gtmp", "psY"), wr=("gtmp",)):
                    P.op("tensor_mul", gtmp[:], gtmp[:], psY[:])
                with P.step("DVE", rd=("gtmp", "xc"), wr=("gtmp",)):
                    P.op("tensor_add", gtmp[:], gtmp[:], xc[:])
                with P.step("POOL", rd=("gtmp",), q="qO"):
                    P.op("dma_start",
                         out=out_d[sl0 + Ti * 128: sl0 + (Ti + 1) * 128, :],
                         in_=gtmp[:])

        skip = cfg.get("skip", ())

        def _empty():
            return
            yield

        for _ in gen_front(0):
            pass
        for s in range(NSLAB):
            serial = gen_serial(s) if "serial" not in skip else _empty()
            backbone = (gen_backbone(s) if "backbone" not in skip
                        else _empty())
            front = gen_front(s + 1) if s + 1 < NSLAB else None
            s_done = b_done = False
            f_done = front is None
            k = 0
            while not (s_done and b_done):
                if not b_done:
                    try:
                        next(backbone)
                    except StopIteration:
                        b_done = True
                nser = 2 if k % 3 == 0 else 1
                for _ in range(nser):
                    if not s_done:
                        try:
                            next(serial)
                        except StopIteration:
                            s_done = True
                k += 1
                if k % 8 == 0 and not f_done:
                    try:
                        next(front)
                    except StopIteration:
                        f_done = True
            while not f_done:
                try:
                    next(front)
                except StopIteration:
                    f_done = True
            if "oproj" not in skip:
                emit_oproj(s)

        for i in range(QLANES["qO"]):
            lane = f"qO{i}"
            tot = P.cnt[lane]
            P.streams["POOL"].append(
                [lambda h, lane=lane, tot=tot:
                 h.wait_ge(P.sems[lane], tot), None])

        block = es.enter_context(nc.Block())

        @block.tensor
        def _(tensor):
            P.replay("PE", tensor)

        @block.vector
        def _(vector):
            P.replay("DVE", vector)

        @block.scalar
        def _(scalar):
            P.replay("ACT", scalar)

        @block.gpsimd
        def _(gpsimd):
            P.replay("POOL", gpsimd)

        @block.sync
        def _(sync):
            P.replay("SP", sync)

    return nc



def make_consts():
    tident = np.eye(128, dtype=np.float32).astype(bf16)
    u128 = (np.arange(128)[None, :] >= np.arange(128)[:, None]).astype(
        np.uint8)
    zbf = np.zeros((128, 128), np.float32).astype(bf16)
    sel16 = np.zeros((H, D), np.float32)
    for hh in range(H):
        sel16[hh, hh * E:(hh + 1) * E] = 1.0
    epsc = np.full((128, 1), 1e-5, np.float32)
    bgate = np.ones((128, SLAB), np.float32)
    bgate[:, ::CH] = 0.0
    return {"tident": tident, "u128": u128, "zbf": zbf, "sel16": sel16,
            "epsc": epsc, "bgate": bgate}


def pack_d(w):
    ng = w.shape[1] // 512
    return [np.ascontiguousarray(
        w[:, h * 512:(h + 1) * 512].reshape(8, 128, 512)
        .transpose(1, 0, 2).reshape(128, 4096)) for h in range(ng)]


def pack_r(w):
    ng = w.shape[0] // 512
    return [np.ascontiguousarray(
        w[h * 512:(h + 1) * 512, :].reshape(4, 128, 1024)
        .transpose(1, 0, 2).reshape(128, 4096)) for h in range(ng)]


def make_inputs(**inputs):
    ct = np.ascontiguousarray
    f = np.asarray
    grans = [None] * NWALL
    for wi, nm in enumerate(("L0_attn_q_w", "L0_attn_k_w", "L0_attn_v_w")):
        gs = pack_d(f(inputs[nm]).T)
        for hw in range(2):
            grans[g_qkv(wi, hw)] = gs[hw]
    gs1 = pack_d(f(inputs["L0_ffn_sw1"]).T)
    gs3 = pack_d(f(inputs["L0_ffn_sw3"]).T)
    for i in range(4):
        grans[g_sw1(i)] = gs1[i]
        grans[g_sw3(i)] = gs3[i]
    gs2 = pack_r(ct(f(inputs["L0_ffn_sw2"]).T))
    for i in range(4):
        grans[g_sw2(i)] = gs2[i]
    w1 = f(inputs["L0_ffn_w1"])
    w3 = f(inputs["L0_ffn_w3"])
    w2 = f(inputs["L0_ffn_w2"])
    for e in range(NE):
        ge1 = pack_d(w1[e].T)
        ge3 = pack_d(w3[e].T)
        ge2 = pack_r(ct(w2[e].T))
        for i in range(2):
            grans[g_we(e, 1, i)] = ge1[i]
            grans[g_we(e, 3, i)] = ge3[i]
            grans[g_we(e, 2, i)] = ge2[i]
    go = pack_d(f(inputs["L0_attn_o_w"]).T)
    for i in range(2):
        grans[g_wo(i)] = go[i]
    wall = np.stack(grans)

    cs1 = f(inputs["L0_attn_cs_w1"]).T
    cs1p = cs1.reshape(8, 128, 256).transpose(1, 0, 2).reshape(128, 2048)
    cs2 = f(inputs["L0_attn_cs_w2"]).T
    cs2p = cs2.reshape(2, 128, 16).transpose(1, 0, 2).reshape(128, 32)
    gw = f(inputs["L0_ffn_gate_w"]).T
    gwp = gw.reshape(8, 128, 8).transpose(1, 0, 2).reshape(128, 64)
    wsm = ct(np.concatenate([cs1p, cs2p, gwp], axis=1))

    m = {"wall": wall, "wsm": wsm}
    m.update(make_consts())
    m["b1c"] = ct(np.asarray(inputs["L0_attn_cs_b1"],
                             np.float32).reshape(2, 128).T)
    m["b2c"] = ct(np.asarray(inputs["L0_attn_cs_b2"],
                             np.float32).reshape(H, 1))
    m["gbias"] = ct(np.tile(np.asarray(inputs["L0_ffn_gate_bias"],
                                       np.float32)[None, :], (128, 1)))
    m["decf"] = ct(np.asarray(inputs["L0_attn_decay_params"],
                              np.float32).reshape(-1).reshape(NC_D, 128).T)
    return m


def kernel(**inputs):
    NCORES = 8
    x = np.ascontiguousarray(np.asarray(inputs["L0_attn_x"], np.float32))
    Bfull, T, _ = x.shape
    B = Bfull // NCORES
    cfg = {"T": T, "B": B}
    wmap = make_inputs(**inputs)
    in_maps = []
    for c in range(NCORES):
        m = dict(wmap)
        m["x"] = x[c * B:(c + 1) * B].reshape(B * T, D)
        in_maps.append(m)
    nc = build(cfg)
    res = run_bass_kernel_spmd(nc, in_maps, core_ids=list(range(NCORES)))
    out = np.concatenate(
        [res.results[c]["out"].reshape(B, T, D) for c in range(NCORES)],
        axis=0)
    return out


# version 10 (final)
# speedup vs baseline: 1.0475x; 1.0378x over previous
import contextlib

import numpy as np
import ml_dtypes

import concourse.bass as bass
import concourse.mybir as mybir
from concourse.bass_utils import run_bass_kernel_spmd

F32 = mybir.dt.float32
BF16 = mybir.dt.bfloat16
U8 = mybir.dt.uint8
AF = mybir.ActivationFunctionType
ALU = mybir.AluOpType
AX = mybir.AxisListType.X
bf16 = ml_dtypes.bfloat16

D = 1024
H = 16
E = 64
DH = 256
NE = 8
INTER = 1024
NSH = 2048
CH = 128
NC_D = D // 128
SLAB = 512
NT = SLAB // 128
NJ = SLAB // CH
NSHC = NSH // 128
NIC = INTER // 128

ACLAMP = 38.0
LN9995 = float(np.log(0.9995))

NWALL = 68


def g_qkv(wi, hw):
    return 2 * wi + hw


def g_sw1(i):
    return 6 + 2 * i


def g_sw3(i):
    return 7 + 2 * i


def g_sw2(i):
    return 14 + i


def g_we(e, kind, i):
    base = 18 + 6 * e
    if kind == 1:
        return base + 2 * i
    if kind == 3:
        return base + 2 * i + 1
    return base + 4 + i


def g_wo(i):
    return 66 + i


ENGS = ("PE", "DVE", "ACT", "POOL", "SP")
QLANES = {"qW": 10, "qX": 3, "qO": 3, "qC": 6, "qT": 4}
QUEUES = tuple(f"{q}{i}" for q, n in QLANES.items() for i in range(n))


class Prog:

    def __init__(self, nc, es, pelag=24):
        self.nc = nc
        self.pelag = pelag
        self.sems = {}
        for name in ENGS + QUEUES:
            self.sems[name] = es.enter_context(nc.semaphore(name))
        self.cnt = {name: 0 for name in ENGS + QUEUES}
        self.qrr = {q: 0 for q in QLANES}
        self.streams = {e: [] for e in ENGS}
        self.last_w = {}
        self.readers = {}
        self.waited = {e: {} for e in ENGS}
        self.mm_total = 0
        self.mm_since_inc = 0
        self.pe_incs = []
        self.pe_wait_idx = 0
        self._cur = None

    def _rec(self, fn):
        self._cur[1].append([fn, None])

    def _mark_inc(self, sem_name, delta):
        assert self._cur[1] and self._cur[1][-1][1] is None
        self.cnt[sem_name] += delta
        self._cur[1][-1][1] = (sem_name, delta)

    def op(self, name, *a, **k):
        self._rec(lambda h: getattr(h, name)(*a, **k))

    def wait(self, sem_name, val):
        eng = self._cur[0]
        if self.waited[eng].get(sem_name, 0) >= val:
            return
        self.waited[eng][sem_name] = val
        s = self.sems[sem_name]
        self._rec(lambda h: h.wait_ge(s, val))

    def mm(self, *a, **k):
        assert self._cur[0] == "PE"
        kind = k.pop("kind", "matmul")
        w = 4 if kind == "transpose" else 1
        while (self.pe_wait_idx < len(self.pe_incs)
               and self.pe_incs[self.pe_wait_idx][1]
               <= self.mm_total - self.pelag):
            self.pe_wait_idx += 1
        if self.pe_wait_idx > 0:
            self.wait("PE", self.pe_incs[self.pe_wait_idx - 1][0])
        self._rec(lambda h: getattr(h, kind)(*a, **k))
        self.mm_total += w
        self.mm_since_inc += w
        if self.mm_since_inc >= 8:
            self._mark_inc("PE", 1)
            self.pe_incs.append((self.cnt["PE"], self.mm_total))
            self.mm_since_inc = 0

    def transpose(self, out, in_, ident):
        self.mm(out, in_, ident, kind="transpose")

    def _deps(self, rd, wr):
        need = {}

        def add(t):
            if t:
                need[t[0]] = max(need.get(t[0], 0), t[1])

        for r in rd:
            add(self.last_w.get(r))
        for r in wr:
            add(self.last_w.get(r))
            for t in self.readers.get(r, ()):
                add(t)
        return need

    @contextlib.contextmanager
    def step(self, eng, rd=(), wr=(), q=None):
        assert self._cur is None
        ops = []
        self._cur = (eng, ops)
        if q:
            lane = f"{q}{self.qrr[q]}"
            self.qrr[q] = (self.qrr[q] + 1) % QLANES[q]
            self.wait(lane, self.cnt[lane])
            sem = lane
        else:
            sem = eng
        for s, v in sorted(self._deps(rd, wr).items()):
            self.wait(s, v)
        yield self
        if ops and ops[-1][1] is None:
            self._mark_inc(sem, 16 if q else 1)
            if eng == "PE" and not q and self.mm_since_inc:
                self.pe_incs.append((self.cnt["PE"], self.mm_total))
                self.mm_since_inc = 0
        token = (sem, self.cnt[sem])
        for r in rd:
            self.readers.setdefault(r, []).append(token)
        for r in wr:
            self.last_w[r] = token
            self.readers[r] = []
        self.streams[eng].extend(ops)
        self._cur = None

    def replay(self, eng, h):
        for fn, inc in self.streams[eng]:
            ins = fn(h)
            if inc is not None:
                ins.then_inc(self.sems[inc[0]], inc[1])


def build(cfg):
    T = cfg["T"]
    B = cfg["B"]
    assert T % SLAB == 0
    NSLAB = T // SLAB * B
    TOK = B * T
    NRING = cfg.get("nring", 5)

    nc = bass.Bass("TRN2", target_bir_lowering=False)

    def din(name, shape, dt=F32):
        return nc.dram_tensor(name, shape, dt, kind="ExternalInput").ap()

    x_d = din("x", [TOK, D])
    wall_d = din("wall", [NWALL, 128, 4096])
    wsm_d = din("wsm", [128, 2048 + 32 + 64])
    c_d = {nm: din(nm, sh, dt) for nm, sh, dt in [
        ("tident", [128, 128], BF16), ("u128", [128, 128], U8),
        ("zbf", [128, 128], BF16), ("sel16", [H, D], F32),
        ("b1c", [128, 2], F32), ("b2c", [H, 1], F32),
        ("gbias", [128, NE], F32), ("epsc", [128, 1], F32),
        ("decf", [128, NC_D], F32), ("bgate", [128, SLAB], F32)]}
    out_d = nc.dram_tensor("out", [TOK, D], F32, kind="ExternalOutput").ap()

    wallb = nc.dram_tensor("wallb", [NWALL, 128, 4096], BF16,
                           kind="Internal").ap()
    wsmb = nc.dram_tensor("wsmb", [128, 2048 + 32 + 64], BF16,
                          kind="Internal").ap()

    es = contextlib.ExitStack()
    with es:
        def sb(name, shape, dt=F32):
            return es.enter_context(nc.sbuf_tensor(name, shape, dt))

        ring = [sb(f"ring{i}", [128, 4096], BF16) for i in range(NRING)]
        xn_fm = [sb(f"xnfm{i}", [128, NC_D * SLAB], BF16) for i in range(2)]
        q_fm = sb("q_fm", [128, NC_D * SLAB], BF16)
        k_fm = sb("k_fm", [128, NC_D * SLAB], BF16)
        v_fm = sb("v_fm", [128, NC_D * SLAB], BF16)
        qa_fm = sb("qa_fm", [128, NC_D * SLAB], BF16)
        ka_fm = sb("ka_fm", [128, NC_D * SLAB], BF16)
        scol_all = sb("scol_all", [128, NC_D * NJ])
        L_fm = sb("L_fm", [128, NC_D * SLAB])
        y_fm = sb("y_fm", [128, NC_D * SLAB], BF16)
        hsh = sb("hsh", [128, NSHC * SLAB], BF16)
        yacc = sb("yacc", [128, NT * D])
        xa = [sb(f"xa{i}", [128, D]) for i in range(2)]
        xc = sb("xc", [128, D])
        sqs = sb("sqs", [128, 512])
        xn_tm = [sb(f"xntm{i}", [128, D], BF16) for i in range(2)]
        z1t = sb("z1t", [128, 2 * SLAB], BF16)
        t2t = sb("t2t", [H, SLAB])
        lamw3 = sb("lamw3", [128, SLAB])
        lam4w = sb("lam4w", [128, 2 * SLAB])
        sa4w = sb("sa4w", [128, 2 * SLAB], BF16)
        col = [[sb(f"col{p}_{i}", [128, 1]) for i in range(8)]
               for p in range(2)]
        kTs = [sb(f"kT{i}", [128, CH], BF16) for i in range(4)]
        vTs = [sb(f"vT{i}", [128, CH], BF16) for i in range(4)]
        am0 = [sb(f"am0_{i}", [128, CH], BF16) for i in range(4)]
        am1 = [sb(f"am1_{i}", [128, CH], BF16) for i in range(4)]
        Sz0 = [sb(f"Sz0_{i}", [128, E], BF16) for i in range(4)]
        Sz1 = [sb(f"Sz1_{i}", [128, E], BF16) for i in range(4)]
        S_fp = [sb(f"S_fp{c}", [128, E]) for c in range(NC_D)]
        hsl2 = [sb(f"hsl{i}", [128, SLAB], BF16) for i in range(2)]
        gtmp = sb("gtmp", [128, D])
        gscr = sb("gscr", [128, NE])
        gat8 = sb("gat8", [128, NE])
        gsb = sb("gsb", [128, NE])
        gmsk = sb("gmsk", [128, NE])
        gm2 = sb("gm2", [128, NE])
        gatdb = sb("gatdb", [128, NT * NE])
        gc = [sb(f"gc{i}", [128, 1]) for i in range(6)]
        cs1r = sb("cs1r", [128, 2048], BF16)
        cs2r = sb("cs2r", [128, 32], BF16)
        gwr = sb("gwr", [128, 64], BF16)
        tident = sb("tidentc", [128, 128], BF16)
        u128 = sb("u128c", [128, 128], U8)
        zbf = sb("zbfc", [128, 128], BF16)
        sel16 = sb("sel16c", [H, D])
        b1c = sb("b1cc", [128, 2])
        b2c = sb("b2cc", [H, 1])
        gbias = sb("gbiasc", [128, NE])
        epsc = sb("epscc", [128, 1])
        decf = sb("decfc", [128, NC_D])
        bgate = sb("bgatec", [128, SLAB])
        basec = sb("basec", [128, NC_D])
        s1c = sb("s1c", [128, NC_D])
        s2c = sb("s2c", [128, NC_D])
        b2h = sb("b2h", [H, 1])

        psP = es.enter_context(nc.psum_tensor("psP", [128, 512], F32))
        psQ = es.enter_context(nc.psum_tensor("psQ", [128, 512], F32))
        psY = es.enter_context(nc.psum_tensor("psY", [128, 1024], F32))
        psL = es.enter_context(nc.psum_tensor("psL", [128, 512], F32))
        psTS = es.enter_context(nc.psum_tensor("psTS", [128, 512], BF16))
        psTB = es.enter_context(nc.psum_tensor("psTB", [128, 512], BF16))
        psAA = es.enter_context(nc.psum_tensor("psAA", [128, 512], F32))
        psA0 = psAA[:, 0:128]
        psA1 = psAA[:, 128:256]
        psO = psAA[:, 256:384]
        psS = psAA[:, 384:448]

        P = Prog(nc, es)

        items = [("tident", tident), ("u128", u128), ("zbf", zbf),
                 ("sel16", sel16), ("b1c", b1c), ("b2c", b2c),
                 ("gbias", gbias), ("epsc", epsc), ("decf", decf),
                 ("bgate", bgate)]
        for nm, tile in items:
            with P.step("SP", wr=(f"const_{nm}",), q="qW"):
                P.op("dma_start", out=tile[:], in_=c_d[nm])
        with P.step("SP", rd=tuple(f"const_{nm}" for nm, _ in items),
                    wr=("consts",)):
            P.op("nop")

        with P.step("ACT", rd=("consts",), wr=("basec",)):
            P.op("activation", basec[:], decf[:], AF.Sigmoid)
        with P.step("DVE", rd=("basec",), wr=("s1c",)):
            P.op("tensor_scalar_mul", s1c[:], basec[:], 0.1)
        with P.step("DVE", rd=("basec",), wr=("s2c",)):
            P.op("tensor_scalar_mul", s2c[:], basec[:], 1.1)
        with P.step("DVE", rd=("consts",), wr=("b2h",)):
            P.op("tensor_scalar_mul", b2h[:], b2c[:], 0.5)
        for r in range(4):
            with P.step("DVE", wr=(f"Sz0_{r}",)):
                P.op("memset", Sz0[r][:], 0.0)
            with P.step("DVE", wr=(f"Sz1_{r}",)):
                P.op("memset", Sz1[r][:], 0.0)

        with P.step("POOL", wr=("cgs",), q="qC"):
            P.op("dma_start", out=wsmb, in_=wsm_d)
        for g in range(NWALL):
            with P.step("POOL", wr=(f"cg{g}",), q="qC"):
                P.op("dma_start", out=wallb[g], in_=wall_d[g])

        with P.step("SP", rd=("cgs",), wr=("cs1r",), q="qW"):
            P.op("dma_start", out=cs1r[:], in_=wsmb[:, 0:2048])
        with P.step("SP", rd=("cgs",), wr=("cs2r",), q="qW"):
            P.op("dma_start", out=cs2r[:], in_=wsmb[:, 2048:2080])
        with P.step("SP", rd=("cgs",), wr=("gwr",), q="qW"):
            P.op("dma_start", out=gwr[:], in_=wsmb[:, 2080:2144])

        ring_i = [0]

        def load_gran(g):
            si = ring_i[0] % NRING
            ring_i[0] += 1
            slot = ring[si]
            keys = (f"rg{si}a", f"rg{si}b")
            for h in range(2):
                with P.step("SP", rd=(f"cg{g}",), wr=(keys[h],), q="qW"):
                    P.op("dma_start", out=slot[h * 64:(h + 1) * 64, :],
                         in_=wallb[g][h * 64:(h + 1) * 64, :])
            return slot, keys

        def fm(tile, c, lo, n):
            return tile[:, c * SLAB + lo: c * SLAB + lo + n]

        def evac(rd, wr, dst_ap, src_ap):
            with P.step("DVE", rd=rd, wr=wr):
                P.op("tensor_copy", dst_ap, src_ap)


        def gen_front(s):
            sp = s % 2
            sl0 = (s // (T // SLAB)) * T + (s % (T // SLAB)) * SLAB
            XW = (f"xn{sp}h0", f"xn{sp}h1")
            for Ti in range(NT):
                p = Ti % 2
                xt = xa[p]
                xkey = f"xa{p}"
                cl = col[p]
                ck = [f"c{p}_{i}" for i in range(8)]
                with P.step("POOL", wr=(xkey,), q="qX"):
                    P.op("dma_start", out=xt[:],
                         in_=x_d[sl0 + Ti * 128: sl0 + (Ti + 1) * 128, :])
                yield
                with P.step("DVE", rd=(xkey,), wr=(ck[0],)):
                    P.op("reduce_sum", cl[0][:], xt[:], axis=AX)
                yield
                with P.step("DVE", rd=(xkey,), wr=("sqs",)):
                    P.op("tensor_mul", sqs[:], xt[:, 0:512], xt[:, 0:512])
                with P.step("DVE", rd=("sqs",), wr=(ck[6],)):
                    P.op("reduce_sum", cl[6][:], sqs[:], axis=AX)
                yield
                with P.step("DVE", rd=(xkey, ck[6]), wr=("sqs",)):
                    P.op("tensor_mul", sqs[:], xt[:, 512:1024],
                         xt[:, 512:1024])
                with P.step("DVE", rd=("sqs",), wr=(ck[7],)):
                    P.op("reduce_sum", cl[7][:], sqs[:], axis=AX)
                yield
                with P.step("DVE", rd=(ck[6], ck[7]), wr=(ck[1],)):
                    P.op("tensor_add", cl[1][:], cl[6][:], cl[7][:])
                yield
                with P.step("DVE", rd=(ck[0],), wr=(ck[2],)):
                    P.op("tensor_scalar_mul", cl[2][:], cl[0][:], 1.0 / D)
                with P.step("DVE", rd=(ck[1],), wr=(ck[3],)):
                    P.op("tensor_scalar_mul", cl[3][:], cl[1][:], 1.0 / D)
                with P.step("DVE", rd=(ck[2],), wr=(ck[4],)):
                    P.op("tensor_mul", cl[4][:], cl[2][:], cl[2][:])
                with P.step("DVE", rd=(ck[3], ck[4]), wr=(ck[5],)):
                    P.op("tensor_sub", cl[5][:], cl[3][:], cl[4][:])
                yield
                with P.step("ACT", rd=(ck[5], "consts"), wr=(ck[6],)):
                    P.op("activation", cl[6][:], cl[5][:], AF.Ln,
                         bias=epsc[:])
                with P.step("ACT", rd=(ck[6],), wr=(ck[7],)):
                    P.op("activation", cl[7][:], cl[6][:], AF.Exp,
                         scale=-0.5)
                yield
                with P.step("DVE", rd=(xkey, ck[2], ck[7]),
                            wr=(f"xntm{p}",)):
                    P.op("tensor_scalar", xn_tm[p][:], xt[:], cl[2][:],
                         cl[7][:], ALU.subtract, ALU.mult)
                yield
                for c in range(NC_D):
                    with P.step("SP", rd=(f"xntm{p}",),
                                wr=(XW[c // 4],), q="qT"):
                        P.op("dma_start",
                             out=xn_fm[sp][:, c * SLAB + Ti * 128:
                                           c * SLAB + Ti * 128 + 128],
                             in_=xn_tm[p][:, c * 128:(c + 1) * 128],
                             transpose=True)
                    if c % 4 == 3:
                        yield

        def gen_serial(s):
            skip = cfg.get("skip", ())
            sp = s % 2
            XN = (f"xn{sp}h0", f"xn{sp}h1")
            if s % (T // SLAB) == 0:
                for c in range(NC_D):
                    with P.step("DVE", wr=(f"S{c}",)):
                        P.op("memset", S_fp[c][:], 0.0)
                yield
            for wi, (dsttile, dkey) in () if "p2" in skip else enumerate(
                    ((q_fm, "q_fm"), (k_fm, "k_fm"), (v_fm, "v_fm"))):
                for hw in range(2):
                    slot, keys = load_gran(g_qkv(wi, hw))
                    yield
                    for cp4 in range(4):
                        cp = hw * 4 + cp4
                        with P.step("PE", rd=keys + XN, wr=("psL",)):
                            for c in range(NC_D):
                                P.mm(psL[:, 0:SLAB],
                                     slot[:, c * 512 + cp4 * 128:
                                          c * 512 + (cp4 + 1) * 128],
                                     fm(xn_fm[sp], c, 0, SLAB),
                                     start=(c == 0), stop=(c == NC_D - 1))
                        yield
                        evac(("psL",), (dkey,), fm(dsttile, cp, 0, SLAB),
                             psL[:, 0:SLAB])
                        yield
            if "p3" in skip:
                return
            for mp in range(2):
                with P.step("PE", rd=("cs1r",) + XN, wr=("psL",)):
                    for c in range(NC_D):
                        P.mm(psL[:, 0:SLAB],
                             cs1r[:, c * 256 + mp * 128:
                                  c * 256 + (mp + 1) * 128],
                             fm(xn_fm[sp], c, 0, SLAB),
                             start=(c == 0), stop=(c == NC_D - 1))
                yield
                with P.step("ACT", rd=("psL", "consts"), wr=(f"z1t{mp}",)):
                    P.op("activation", z1t[:, mp * SLAB:(mp + 1) * SLAB],
                         psL[:, 0:SLAB], AF.Tanh, bias=b1c[:, mp:mp + 1])
                yield
            with P.step("PE", rd=("cs2r", "z1t0", "z1t1"), wr=("psL",)):
                for c2 in range(2):
                    P.mm(psL[0:H, 0:SLAB], cs2r[:, c2 * H:(c2 + 1) * H],
                         z1t[:, c2 * SLAB:(c2 + 1) * SLAB],
                         start=(c2 == 0), stop=(c2 == 1))
            yield
            with P.step("ACT", rd=("psL", "b2h"), wr=("t2t",)):
                P.op("activation", t2t[:], psL[0:H, 0:SLAB], AF.Tanh,
                     bias=b2h[:], scale=0.5)
            yield
            for c in range(NC_D):
                with P.step("PE", rd=("t2t", "consts"), wr=("psL",)):
                    P.mm(psL[:, 0:SLAB], sel16[:, c * 128:(c + 1) * 128],
                         t2t[:], start=True, stop=True)
                yield
                with P.step("DVE", rd=("psL", "s1c", "s2c"), wr=("lamw3",)):
                    P.op("tensor_scalar", lamw3[:], psL[:, 0:SLAB],
                         s1c[:, c:c + 1], s2c[:, c:c + 1],
                         ALU.mult, ALU.add)
                yield
                with P.step("ACT", rd=("lamw3",), wr=("lamw3",)):
                    P.op("activation", lamw3[:], lamw3[:], AF.Ln)
                yield
                with P.step("DVE", rd=("lamw3",), wr=("lamw3",)):
                    P.op("tensor_scalar_min", lamw3[:], lamw3[:], LN9995)
                yield
                with P.step("DVE", rd=("lamw3", "consts"), wr=(f"L{c}",)):
                    P.op("tensor_tensor_scan", fm(L_fm, c, 0, SLAB),
                         bgate[:], lamw3[:], 0.0, ALU.mult, ALU.add)
                yield
            if "p45" in skip:
                return
            for c in range(NC_D):
                Lc = fm(L_fm, c, 0, SLAB)
                Lv = Lc.rearrange("p (j u) -> p j u", j=NJ)
                bc_mid = bass.AP(L_fm, c * SLAB + CH // 2 - 1,
                                 [[NC_D * SLAB, 128], [CH, NJ], [0, CH]])
                bc_end = bass.AP(L_fm, c * SLAB + CH - 1,
                                 [[NC_D * SLAB, 128], [CH, NJ], [0, CH]])
                lamv = lam4w[:, 0:SLAB].rearrange("p (j u) -> p j u", j=NJ)
                with P.step("DVE", rd=(f"L{c}",), wr=("lam4",)):
                    P.op("tensor_tensor", lamv, Lv, bc_mid, ALU.subtract)
                yield
                with P.step("DVE", rd=("lam4",), wr=("lam4",)):
                    P.op("tensor_scalar", lam4w[:, 0:SLAB], lam4w[:, 0:SLAB],
                         ACLAMP, -ACLAMP, ALU.min, ALU.max)
                yield
                with P.step("DVE", rd=("lam4",), wr=("lam4",)):
                    P.op("tensor_scalar_mul", lam4w[:, SLAB:2 * SLAB],
                         lam4w[:, 0:SLAB], -1.0)
                yield
                with P.step("ACT", rd=("lam4",), wr=("sa4",)):
                    P.op("activation", sa4w[:], lam4w[:], AF.Exp)
                yield
                with P.step("DVE", rd=("sa4", "q_fm"), wr=("qa_fm",)):
                    P.op("tensor_mul", fm(qa_fm, c, 0, SLAB),
                         fm(q_fm, c, 0, SLAB), sa4w[:, 0:SLAB])
                yield
                with P.step("DVE", rd=("sa4", "k_fm"), wr=("ka_fm",)):
                    P.op("tensor_mul", fm(ka_fm, c, 0, SLAB),
                         fm(k_fm, c, 0, SLAB), sa4w[:, SLAB:2 * SLAB])
                yield
                lamv2 = lam4w[:, 0:SLAB].rearrange("p (j u) -> p j u", j=NJ)
                with P.step("DVE", rd=(f"L{c}",), wr=("lam4",)):
                    P.op("tensor_tensor", lamv2, bc_end, Lv, ALU.subtract)
                yield
                with P.step("DVE", rd=(f"L{c}",), wr=("lam4",)):
                    P.op("tensor_copy", lam4w[:, SLAB:2 * SLAB], Lc)
                yield
                with P.step("ACT", rd=("lam4",), wr=("sa4",)):
                    P.op("activation", sa4w[:], lam4w[:], AF.Exp)
                yield
                with P.step("DVE", rd=("sa4", "k_fm"), wr=("k_fm",)):
                    P.op("tensor_mul", fm(k_fm, c, 0, SLAB),
                         fm(k_fm, c, 0, SLAB), sa4w[:, 0:SLAB])
                yield
                with P.step("DVE", rd=("sa4", "q_fm"), wr=("q_fm",)):
                    P.op("tensor_mul", fm(q_fm, c, 0, SLAB),
                         fm(q_fm, c, 0, SLAB), sa4w[:, SLAB:2 * SLAB])
                yield
            ends = bass.AP(L_fm, CH - 1,
                           [[NC_D * SLAB, 128], [SLAB, NC_D], [CH, NJ]])
            with P.step("ACT", rd=tuple(f"L{c}" for c in range(NC_D)),
                        wr=("scol",)):
                P.op("activation",
                     scol_all[:].rearrange("p (c j) -> p c j", c=NC_D),
                     ends, AF.Exp)
            yield
            def scan_chunk(c, ci, pA0, pA1, pO, pS, bk, pT, ptk):
                for j in () if "p5" in skip else range(NJ):
                    lo = j * CH
                    rb = 2 * ci + (j % 2)
                    with P.step("PE", rd=("k_fm", "consts"), wr=(ptk,)):
                        P.transpose(pT[:, 0:CH], fm(k_fm, c, lo, CH),
                                    tident[:])
                    yield
                    with P.step("DVE", rd=(ptk,), wr=(f"kT{rb}",)):
                        P.op("tensor_copy", kTs[rb][:], pT[:, 0:CH])
                    yield
                    with P.step("PE", rd=("v_fm", "consts"), wr=(ptk,)):
                        P.transpose(pT[:, CH:2 * CH], fm(v_fm, c, lo, CH),
                                    tident[:])
                    yield
                    with P.step("DVE", rd=(ptk,), wr=(f"vT{rb}",)):
                        P.op("tensor_copy", vTs[rb][:], pT[:, CH:2 * CH])
                    yield
                    with P.step("PE", rd=("qa_fm", "ka_fm"), wr=(bk,)):
                        P.mm(pA0, fm(ka_fm, c, lo, CH)[0:64, :],
                             fm(qa_fm, c, lo, CH)[0:64, :],
                             start=True, stop=True)
                    yield
                    with P.step("PE", rd=("qa_fm", "ka_fm"), wr=(bk,)):
                        P.mm(pA1, fm(ka_fm, c, lo, CH)[64:128, :],
                             fm(qa_fm, c, lo, CH)[64:128, :],
                             start=True, stop=True)
                    yield
                    with P.step("DVE", rd=(bk, "consts"),
                                wr=(f"am0_{rb}",)):
                        P.op("select", am0[rb][:], u128[:], pA0, zbf[:],
                             add_drain=True)
                    yield
                    with P.step("DVE", rd=(bk, "consts"),
                                wr=(f"am1_{rb}",)):
                        P.op("select", am1[rb][:], u128[:], pA1, zbf[:],
                             add_drain=True)
                    yield
                    with P.step("DVE", rd=(f"S{c}",), wr=(f"Sz0_{rb}",)):
                        P.op("tensor_copy", Sz0[rb][0:64, :],
                             S_fp[c][0:64, :])
                    with P.step("DVE", rd=(f"S{c}",), wr=(f"Sz1_{rb}",)):
                        P.op("tensor_copy", Sz1[rb][64:128, :],
                             S_fp[c][64:128, :])
                    yield
                    with P.step("PE", rd=(f"vT{rb}", f"am0_{rb}",
                                          f"am1_{rb}", f"Sz0_{rb}",
                                          f"Sz1_{rb}", "q_fm"), wr=(bk,)):
                        P.mm(pO[0:64, :], vTs[rb][:, 0:64], am0[rb][:],
                             start=True, stop=False, tile_position=(0, 0))
                        P.mm(pO[0:64, :], Sz0[rb][:], fm(q_fm, c, lo, CH),
                             start=False, stop=True, tile_position=(0, 0))
                        P.mm(pO[64:128, :], vTs[rb][:, 64:128],
                             am1[rb][:], start=True, stop=False,
                             tile_position=(0, 64))
                        P.mm(pO[64:128, :], Sz1[rb][:],
                             fm(q_fm, c, lo, CH), start=False, stop=True,
                             tile_position=(0, 64))
                    yield
                    with P.step("DVE", rd=(bk,), wr=("y_fm",)):
                        P.op("tensor_copy", fm(y_fm, c, lo, CH), pO)
                    yield
                    with P.step("PE", rd=(f"kT{rb}", f"vT{rb}"), wr=(bk,)):
                        P.mm(pS[0:64, :], kTs[rb][:, 0:64],
                             vTs[rb][:, 0:64], start=True, stop=True,
                             tile_position=(0, 0))
                        P.mm(pS[64:128, :], kTs[rb][:, 64:128],
                             vTs[rb][:, 64:128], start=True, stop=True,
                             tile_position=(0, 64))
                    yield
                    with P.step("DVE", rd=(bk, "scol", f"S{c}"),
                                wr=(f"S{c}",)):
                        P.op("scalar_tensor_tensor", S_fp[c][:],
                             S_fp[c][:],
                             scol_all[:, c * NJ + j:c * NJ + j + 1], pS,
                             ALU.mult, ALU.add)
                    yield

            for cp in range(NC_D // 2):
                gA = scan_chunk(2 * cp, 0, psA0, psA1, psO, psS,
                                "psAA", psTS, "psTS")
                if cfg.get("single_chain"):
                    alive = [gA]
                else:
                    gB = scan_chunk(2 * cp + 1, 1, psL[:, 0:128],
                                    psL[:, 128:256], psL[:, 256:384],
                                    psL[:, 384:448], "psL", psTB, "psTB")
                    alive = [gA, gB]
                while alive:
                    for g in list(alive):
                        try:
                            next(g)
                        except StopIteration:
                            alive.remove(g)
                            continue
                        yield

        def gen_gates(s):
            sp = s % 2
            XN = (f"xn{sp}h0", f"xn{sp}h1")
            for Ti in range(NT):
                with P.step("PE", rd=("gwr",) + XN, wr=("psY",)):
                    for c in range(NC_D):
                        P.mm(psY[:, 0:NE],
                             fm(xn_fm[sp], c, Ti * 128, 128),
                             gwr[:, c * NE:(c + 1) * NE],
                             start=(c == 0), stop=(c == NC_D - 1))
                yield
                with P.step("DVE", rd=("psY",), wr=("gc0",)):
                    P.op("reduce_max", gc[0][:], psY[:, 0:NE], axis=AX)
                yield
                with P.step("DVE", rd=("gc0",), wr=("gc1",)):
                    P.op("tensor_scalar_mul", gc[1][:], gc[0][:], -1.0)
                yield
                with P.step("ACT", rd=("psY", "gc1"), wr=("gscr",)):
                    P.op("activation", gscr[:], psY[:, 0:NE], AF.Exp,
                         bias=gc[1][:])
                yield
                with P.step("DVE", rd=("gscr",), wr=("gc2",)):
                    P.op("reduce_sum", gc[2][:], gscr[:], axis=AX)
                yield
                with P.step("DVE", rd=("gc2",), wr=("gc3",)):
                    P.op("reciprocal", gc[3][:], gc[2][:])
                yield
                with P.step("DVE", rd=("gscr", "gc3"), wr=("gat8",)):
                    P.op("tensor_scalar_mul", gat8[:], gscr[:], gc[3][:])
                yield
                with P.step("DVE", rd=("gat8", "consts"), wr=("gsb",)):
                    P.op("tensor_add", gsb[:], gat8[:], gbias[:])
                yield
                with P.step("DVE", rd=("gsb",), wr=("gc4",)):
                    P.op("reduce_max", gc[4][:], gsb[:], axis=AX)
                yield
                with P.step("DVE", rd=("gsb", "gc4"), wr=("gmsk",)):
                    P.op("tensor_single_scalar", gmsk[:], gsb[:],
                         gc[4][:], ALU.is_ge)
                yield
                with P.step("DVE", rd=("gmsk", "gsb"), wr=("gm2",)):
                    P.op("scalar_tensor_tensor", gm2[:], gmsk[:],
                         -1e9, gsb[:], ALU.mult, ALU.add)
                yield
                with P.step("DVE", rd=("gm2",), wr=("gc5",)):
                    P.op("reduce_max", gc[5][:], gm2[:], axis=AX)
                yield
                with P.step("DVE", rd=("gsb", "gc5"), wr=("gmsk",)):
                    P.op("tensor_single_scalar", gmsk[:], gsb[:],
                         gc[5][:], ALU.is_ge)
                yield
                with P.step("DVE", rd=("gmsk", "gat8"), wr=(f"gat{Ti}",)):
                    P.op("tensor_mul", gatdb[:, Ti * NE:(Ti + 1) * NE],
                         gmsk[:], gat8[:])
                yield

        def gen_backbone(s):
            sp = s % 2
            XN = (f"xn{sp}h0", f"xn{sp}h1")
            gates = gen_gates(s)

            def produce_h(m, slotA, keysA, mA, slotB, keysB, mB):
                pp, ppk, qq, qqk = (psP, "psP", psQ, "psQ")
                hb = hsl2[m % 2]
                hbk = f"hsl{m % 2}"
                with P.step("PE", rd=keysA + XN, wr=(ppk,)):
                    for c in range(NC_D):
                        P.mm(pp[:, 0:SLAB],
                             slotA[:, c * 512 + mA * 128:
                                   c * 512 + (mA + 1) * 128],
                             fm(xn_fm[sp], c, 0, SLAB),
                             start=(c == 0), stop=(c == NC_D - 1))
                yield
                with P.step("PE", rd=keysB + XN, wr=(qqk,)):
                    for c in range(NC_D):
                        P.mm(qq[:, 0:SLAB],
                             slotB[:, c * 512 + mB * 128:
                                   c * 512 + (mB + 1) * 128],
                             fm(xn_fm[sp], c, 0, SLAB),
                             start=(c == 0), stop=(c == NC_D - 1))
                yield
                with P.step("ACT", rd=(ppk,), wr=(hbk,)):
                    P.op("activation", hb[:], pp[:, 0:SLAB], AF.Silu)
                yield
                with P.step("DVE", rd=(hbk, qqk), wr=(f"hsh{m}",)):
                    P.op("tensor_mul", hsh[:, m * SLAB:(m + 1) * SLAB],
                         hb[:], qq[:, 0:SLAB])
                yield

            def consume_h(w2slots, mchunks, gate_e):
                nslots = len(w2slots)
                for Ti in range(NT):
                    rdk = tuple(f"hsh{m}" for m in range(mchunks))
                    for _, ks in w2slots:
                        rdk = rdk + ks
                    with P.step("PE", rd=rdk, wr=("psY",)):
                        for m in range(mchunks):
                            gsl, _ = w2slots[m * nslots // mchunks]
                            m_in = m % (mchunks // nslots)
                            for half in range(2):
                                P.mm(psY[:, half * 512:(half + 1) * 512],
                                     hsh[:, m * SLAB + Ti * 128:
                                         m * SLAB + Ti * 128 + 128],
                                     gsl[:, m_in * D + half * 512:
                                         m_in * D + (half + 1) * 512],
                                     start=(m == 0),
                                     stop=(m == mchunks - 1))
                    yield
                    if gate_e is None:
                        with P.step("DVE", rd=("psY",), wr=(f"yacc{Ti}",)):
                            P.op("tensor_copy",
                                 yacc[:, Ti * D:(Ti + 1) * D], psY[:])
                    else:
                        gcol = gatdb[:, Ti * NE + gate_e:
                                     Ti * NE + gate_e + 1]
                        with P.step("DVE", rd=("psY", f"gat{Ti}",
                                               f"yacc{Ti}"),
                                    wr=(f"yacc{Ti}",)):
                            P.op("scalar_tensor_tensor",
                                 yacc[:, Ti * D:(Ti + 1) * D], psY[:],
                                 gcol, yacc[:, Ti * D:(Ti + 1) * D],
                                 ALU.mult, ALU.add)
                    yield

            gates_done = [False]

            def adv_gates(n):
                if gates_done[0]:
                    return
                for _ in range(n):
                    try:
                        next(gates)
                    except StopIteration:
                        gates_done[0] = True
                        return

            for i in range(4):
                sA, kA = load_gran(g_sw1(i))
                sB, kB = load_gran(g_sw3(i))
                yield
                for m4 in range(4):
                    for _ in produce_h(i * 4 + m4, sA, kA, m4, sB, kB, m4):
                        adv_gates(1)
                        yield
            sw2sl = []
            for i in range(4):
                sl, ks = load_gran(g_sw2(i))
                sw2sl.append((sl, ks))
                yield
            for _ in consume_h(sw2sl, NSHC, None):
                adv_gates(2)
                yield
            for e in range(NE):
                esl = []
                for i in range(2):
                    sA, kA = load_gran(g_we(e, 1, i))
                    sB, kB = load_gran(g_we(e, 3, i))
                    yield
                    for m4 in range(4):
                        for _ in produce_h(i * 4 + m4, sA, kA, m4,
                                           sB, kB, m4):
                            adv_gates(1)
                            yield
                for i in range(2):
                    sl, ks = load_gran(g_we(e, 2, i))
                    esl.append((sl, ks))
                    yield
                for _ in consume_h(esl, NIC, e):
                    yield
            adv_gates(100)

        def emit_oproj(s):
            sl0 = (s // (T // SLAB)) * T + (s % (T // SLAB)) * SLAB
            wosl = [load_gran(g_wo(i)) for i in range(2)]
            for Ti in range(NT):
                rdk = ("y_fm",) + wosl[0][1] + wosl[1][1]
                with P.step("PE", rd=rdk, wr=("psY",)):
                    for c in range(NC_D):
                        for half in range(2):
                            P.mm(psY[:, half * 512:(half + 1) * 512],
                                 fm(y_fm, c, Ti * 128, 128),
                                 wosl[half][0][:, c * 512:(c + 1) * 512],
                                 start=(c == 0), stop=(c == NC_D - 1))
                with P.step("POOL", wr=("xc",), q="qX"):
                    P.op("dma_start", out=xc[:],
                         in_=x_d[sl0 + Ti * 128: sl0 + (Ti + 1) * 128, :])
                with P.step("ACT", rd=(f"yacc{Ti}",), wr=("gtmp",)):
                    P.op("activation", gtmp[:],
                         yacc[:, Ti * D:(Ti + 1) * D], AF.Silu)
                with P.step("DVE", rd=("gtmp", "psY"), wr=("gtmp",)):
                    P.op("tensor_mul", gtmp[:], gtmp[:], psY[:])
                with P.step("DVE", rd=("gtmp", "xc"), wr=("gtmp",)):
                    P.op("tensor_add", gtmp[:], gtmp[:], xc[:])
                with P.step("POOL", rd=("gtmp",), q="qO"):
                    P.op("dma_start",
                         out=out_d[sl0 + Ti * 128: sl0 + (Ti + 1) * 128, :],
                         in_=gtmp[:])

        skip = cfg.get("skip", ())

        def _empty():
            return
            yield

        for _ in gen_front(0):
            pass
        for s in range(NSLAB):
            serial = gen_serial(s) if "serial" not in skip else _empty()
            backbone = (gen_backbone(s) if "backbone" not in skip
                        else _empty())
            front = gen_front(s + 1) if s + 1 < NSLAB else None
            s_done = b_done = False
            f_done = front is None
            k = 0
            while not (s_done and b_done):
                if not b_done:
                    try:
                        next(backbone)
                    except StopIteration:
                        b_done = True
                nser = 2 if k % 3 == 0 else 1
                for _ in range(nser):
                    if not s_done:
                        try:
                            next(serial)
                        except StopIteration:
                            s_done = True
                k += 1
                if k % 8 == 0 and not f_done:
                    try:
                        next(front)
                    except StopIteration:
                        f_done = True
            while not f_done:
                try:
                    next(front)
                except StopIteration:
                    f_done = True
            if "oproj" not in skip:
                emit_oproj(s)

        for i in range(QLANES["qO"]):
            lane = f"qO{i}"
            tot = P.cnt[lane]
            P.streams["POOL"].append(
                [lambda h, lane=lane, tot=tot:
                 h.wait_ge(P.sems[lane], tot), None])

        block = es.enter_context(nc.Block())

        @block.tensor
        def _(tensor):
            P.replay("PE", tensor)

        @block.vector
        def _(vector):
            P.replay("DVE", vector)

        @block.scalar
        def _(scalar):
            P.replay("ACT", scalar)

        @block.gpsimd
        def _(gpsimd):
            P.replay("POOL", gpsimd)

        @block.sync
        def _(sync):
            P.replay("SP", sync)

    return nc



def make_consts():
    tident = np.eye(128, dtype=np.float32).astype(bf16)
    u128 = (np.arange(128)[None, :] >= np.arange(128)[:, None]).astype(
        np.uint8)
    zbf = np.zeros((128, 128), np.float32).astype(bf16)
    sel16 = np.zeros((H, D), np.float32)
    for hh in range(H):
        sel16[hh, hh * E:(hh + 1) * E] = 1.0
    epsc = np.full((128, 1), 1e-5, np.float32)
    bgate = np.ones((128, SLAB), np.float32)
    bgate[:, ::CH] = 0.0
    return {"tident": tident, "u128": u128, "zbf": zbf, "sel16": sel16,
            "epsc": epsc, "bgate": bgate}


def pack_d(w):
    ng = w.shape[1] // 512
    return [np.ascontiguousarray(
        w[:, h * 512:(h + 1) * 512].reshape(8, 128, 512)
        .transpose(1, 0, 2).reshape(128, 4096)) for h in range(ng)]


def pack_r(w):
    ng = w.shape[0] // 512
    return [np.ascontiguousarray(
        w[h * 512:(h + 1) * 512, :].reshape(4, 128, 1024)
        .transpose(1, 0, 2).reshape(128, 4096)) for h in range(ng)]


def make_inputs(**inputs):
    ct = np.ascontiguousarray
    f = np.asarray
    grans = [None] * NWALL
    for wi, nm in enumerate(("L0_attn_q_w", "L0_attn_k_w", "L0_attn_v_w")):
        gs = pack_d(f(inputs[nm]).T)
        for hw in range(2):
            grans[g_qkv(wi, hw)] = gs[hw]
    gs1 = pack_d(f(inputs["L0_ffn_sw1"]).T)
    gs3 = pack_d(f(inputs["L0_ffn_sw3"]).T)
    for i in range(4):
        grans[g_sw1(i)] = gs1[i]
        grans[g_sw3(i)] = gs3[i]
    gs2 = pack_r(ct(f(inputs["L0_ffn_sw2"]).T))
    for i in range(4):
        grans[g_sw2(i)] = gs2[i]
    w1 = f(inputs["L0_ffn_w1"])
    w3 = f(inputs["L0_ffn_w3"])
    w2 = f(inputs["L0_ffn_w2"])
    for e in range(NE):
        ge1 = pack_d(w1[e].T)
        ge3 = pack_d(w3[e].T)
        ge2 = pack_r(ct(w2[e].T))
        for i in range(2):
            grans[g_we(e, 1, i)] = ge1[i]
            grans[g_we(e, 3, i)] = ge3[i]
            grans[g_we(e, 2, i)] = ge2[i]
    go = pack_d(f(inputs["L0_attn_o_w"]).T)
    for i in range(2):
        grans[g_wo(i)] = go[i]
    wall = np.stack(grans)

    cs1 = f(inputs["L0_attn_cs_w1"]).T
    cs1p = cs1.reshape(8, 128, 256).transpose(1, 0, 2).reshape(128, 2048)
    cs2 = f(inputs["L0_attn_cs_w2"]).T
    cs2p = cs2.reshape(2, 128, 16).transpose(1, 0, 2).reshape(128, 32)
    gw = f(inputs["L0_ffn_gate_w"]).T
    gwp = gw.reshape(8, 128, 8).transpose(1, 0, 2).reshape(128, 64)
    wsm = ct(np.concatenate([cs1p, cs2p, gwp], axis=1))

    m = {"wall": wall, "wsm": wsm}
    m.update(make_consts())
    m["b1c"] = ct(np.asarray(inputs["L0_attn_cs_b1"],
                             np.float32).reshape(2, 128).T)
    m["b2c"] = ct(np.asarray(inputs["L0_attn_cs_b2"],
                             np.float32).reshape(H, 1))
    m["gbias"] = ct(np.tile(np.asarray(inputs["L0_ffn_gate_bias"],
                                       np.float32)[None, :], (128, 1)))
    m["decf"] = ct(np.asarray(inputs["L0_attn_decay_params"],
                              np.float32).reshape(-1).reshape(NC_D, 128).T)
    return m


def kernel(**inputs):
    NCORES = 8
    x = np.ascontiguousarray(np.asarray(inputs["L0_attn_x"], np.float32))
    Bfull, T, _ = x.shape
    B = Bfull // NCORES
    cfg = {"T": T, "B": B}
    wmap = make_inputs(**inputs)
    in_maps = []
    for c in range(NCORES):
        m = dict(wmap)
        m["x"] = x[c * B:(c + 1) * B].reshape(B * T, D)
        in_maps.append(m)
    nc = build(cfg)
    res = run_bass_kernel_spmd(nc, in_maps, core_ids=list(range(NCORES)))
    out = np.concatenate(
        [res.results[c]["out"].reshape(B, T, D) for c in range(NCORES)],
        axis=0)
    return out
